# version 23
# baseline (speedup 1.0000x reference)
import numpy as np
import ml_dtypes
from contextlib import ExitStack

import concourse.bass as bass
import concourse.mybir as mybir
from concourse.bass_utils import run_bass_kernel_spmd

F32 = mybir.dt.float32
BF16 = mybir.dt.bfloat16
I32 = mybir.dt.int32
AF = mybir.ActivationFunctionType
ALU = mybir.AluOpType
AX = mybir.AxisListType

D = 4096
TOK = 1024
EPS = 1e-5


class Sem:
    __slots__ = ("h", "n")

    def __init__(self, h):
        self.h = h
        self.n = 0

    def dma(self, instr):
        instr.then_inc(self.h, 16)
        self.n += 16
        return self.n

    def inc(self, instr):
        instr.then_inc(self.h, 1)
        self.n += 1
        return self.n


def build_nc():
    nc = bass.Bass("TRN2", target_bir_lowering=False)

    x_in = nc.dram_tensor("x_in", [TOK, D], F32, kind="ExternalInput").ap()
    mask = nc.dram_tensor("mask", [D, D], mybir.dt.int8, kind="ExternalInput").ap()
    wt_t = nc.dram_tensor("wt_t", [16, 128, 8192], F32, kind="ExternalInput")
    qw = nc.dram_tensor("qw", [D, D], mybir.dt.int16, kind="ExternalInput").ap()
    scl = nc.dram_tensor("scl", [128, 1024], F32, kind="ExternalInput").ap()
    zr = nc.dram_tensor("zr", [32, D], F32, kind="ExternalInput").ap()
    ab = nc.dram_tensor("ab", [1, D], F32, kind="ExternalInput").ap()
    fb = nc.dram_tensor("fb", [1, D], F32, kind="ExternalInput").ap()
    idn = nc.dram_tensor("idn", [128, 128], BF16, kind="ExternalInput").ap()
    idnh = nc.dram_tensor("idnh", [128, 128], BF16, kind="ExternalInput").ap()
    nbsel_d = nc.dram_tensor("nbsel", [128, 256], BF16, kind="ExternalInput").ap()
    selpad_d = nc.dram_tensor("selpad", [128, D], BF16, kind="ExternalInput").ap()
    y = nc.dram_tensor("y", [TOK, D], F32, kind="ExternalOutput").ap()
    scr_f = nc.dram_tensor("scr_f", [16, 128, 8192], BF16, kind="Internal")
    scr_a = nc.dram_tensor("scr_a", [16, 128, 8192], BF16, kind="Internal")

    st = ExitStack()
    E = st.enter_context
    R = E(nc.sbuf_tensor("R", [128, 8 * D], F32))
    xnT = E(nc.sbuf_tensor("xnT", [128, 32 * 512], BF16))
    arena = [E(nc.sbuf_tensor(f"slab{i}", [128, 2048], BF16)) for i in range(3)]
    xnb = [E(nc.sbuf_tensor(f"xnb{i}", [128, 2048], BF16)) for i in range(2)]
    bias_sb = E(nc.sbuf_tensor("bias_sb", [1, D], BF16))
    idn_sb = E(nc.sbuf_tensor("idn_sb", [128, 128], BF16))
    idnh_sb = E(nc.sbuf_tensor("idnh_sb", [128, 128], BF16))
    nbsel_sb = E(nc.sbuf_tensor("nbsel_sb", [128, 256], BF16))
    halfc = E(nc.sbuf_tensor("halfc", [1, 128], BF16))
    onec = E(nc.sbuf_tensor("onec", [1, 128], BF16))
    S1 = [E(nc.sbuf_tensor("S1a", [128, 128], BF16))]
    bm_bf = E(nc.sbuf_tensor("bm_bf", [128, 128], BF16))
    bmT = E(nc.sbuf_tensor("bmT", [128, 128], BF16))
    mv = E(nc.sbuf_tensor("mv", [128, 16], F32))
    stats6 = E(nc.sbuf_tensor("stats6", [128, 192], F32))
    istd = E(nc.sbuf_tensor("istd", [128, 8], F32))
    nmi = E(nc.sbuf_tensor("nmi", [128, 8], F32))

    psMM = E(nc.psum_tensor("psMM", [128, 2048], F32))
    psDQ = E(nc.psum_tensor("psDQ", [128, 1536], F32))
    psBM = E(nc.psum_tensor("psBM", [128, 512], F32))

    sQbf = [Sem(E(nc.semaphore(f"sQbf{i}"))) for i in range(3)]
    sZr = [Sem(E(nc.semaphore(f"sZr{i}"))) for i in range(2)]
    sWtb = [Sem(E(nc.semaphore(f"sWtb{i}"))) for i in range(3)]
    sGm = Sem(E(nc.semaphore("sGm")))
    sArena = [Sem(E(nc.semaphore(f"sAr{i}"))) for i in range(3)]
    sAr2 = [Sem(E(nc.semaphore(f"sAq{i}"))) for i in range(3)]
    sMb = [Sem(E(nc.semaphore(f"sMb{i}"))) for i in range(2)]
    sXY = Sem(E(nc.semaphore("sXY")))
    sCst = Sem(E(nc.semaphore("sCst")))
    sWr3 = [Sem(E(nc.semaphore(f"sWr{i}"))) for i in range(3)]
    sWr_a = [Sem(E(nc.semaphore(f"sWa{i}"))) for i in range(3)]
    sSelp = [Sem(E(nc.semaphore(f"sSp{i}"))) for i in range(2)]
    sScl = Sem(E(nc.semaphore("sScl")))
    sBias = Sem(E(nc.semaphore("sBias")))
    sX2 = Sem(E(nc.semaphore("sX2")))
    eT = Sem(E(nc.semaphore("eT")))
    eV = Sem(E(nc.semaphore("eV")))
    eA = Sem(E(nc.semaphore("eA")))

    hwm = {}

    def W(eng, key, sem, val):
        if val is None or val <= 0:
            return
        k = (key, id(sem))
        if val > hwm.get(k, 0):
            eng.wait_ge(sem.h, val)
            hwm[k] = val

    xload = [0] * 8
    idn_load = [0]
    bias_load = [0]
    chunk_cnt = [0] * 8
    lane_free = [0] * 4
    stats_done = [0] * 8
    istd_done = [0] * 8
    nmi_done = [0] * 8
    apply_done = [0] * 8
    xnb_guard = [0, 0]
    ps_free = [0, 0]
    dq_free = [(eV, 0), (eV, 0), (eV, 0)]
    row_done = [0] * 8
    qbf_load = {}
    qbf_guard = [0] * 3
    zrow_load = {}
    zrow_guard = [0, 0]
    diag_ready = [None] * 2
    diag_guard = [0] * 2
    wr3_guard = [0, 0, 0]
    arena_rdy = {}
    radd = {}
    wr_f = {}
    wr_a = {}
    wr_a_guard = [0, 0, 0]
    mult_done = {}
    m_load = {}
    m_red = [0] * 128
    s1_done = [0] * 32
    bm_mm = [0] * 32
    n_mask_emitted = [0]
    n_red_emitted = [0]
    bm_final = [0]
    wtb_load = {}
    n_aq = [0]
    slab_load = {}
    slab_guard = [0, 0, 0]
    mm_done = {}
    tr_mm = {}
    dq_mm = {}
    dq_slots = []
    dqb = [0]
    bank_seq = [0]
    q_rr = [0]
    z_rr = [0]
    d_rr = [0]
    pg_seq = [0]
    slab_rr = [0]
    selp_ld = [0] * 32
    selp_mm = [0] * 32
    selp_bank = [0] * 32
    bmtr = [None]
    scope_guard = {}
    holder = {}

    def BLK():
        return holder["blk"]


    def emit_init():
        blk = BLK()

        def f_sync(sync):
            sCst.dma(sync.dma_start(out=idn_sb[:], in_=idn))
            sCst.dma(sync.dma_start(out=idnh_sb[:], in_=idnh))
            sCst.dma(sync.dma_start(out=nbsel_sb[:], in_=nbsel_d))
            idn_load[0] = sCst.n
            for t in range(4):
                sXY.dma(sync.dma_start(
                    out=bass.AP(R, t * D, [[8 * D, 128], [1, D]]),
                    in_=x_in[128 * t:128 * (t + 1), :]))
            for t in range(4):
                xload[t] = (sXY, sXY.n)
        blk.sync(f_sync)

        def f_sc(s):
            for t in range(4, 8):
                sX2.dma(s.dma_start(
                    out=bass.AP(R, t * D, [[8 * D, 128], [1, D]]),
                    in_=x_in[128 * t:128 * (t + 1), :]))
            for t in range(4, 8):
                xload[t] = (sX2, sX2.n)
        blk.scalar(f_sc)

        def f_gp(g):
            sBias.dma(g.dma_start(out=bias_sb[:], in_=fb))
            bias_load[0] = sBias.n
        blk.gpsimd(f_gp)

        def f_vec(v):
            eV.inc(v.memset(halfc[:], 0.5))
            eV.inc(v.memset(onec[:], 1.0))
        blk.vector(f_vec)
        return sScl.n

    def emit_stats_chunk(t, c0, c1, gate):
        blk = BLK()

        def f_c(v, t=t, c0=c0, c1=c1, gate=gate):
            if gate is not None:
                W(v, "v", gate[0], gate[1])
            if c0 == 0:
                W(v, "v", eV, lane_free[t % 4])
            lane = (t % 4) * 48
            for c in range(c0, c1):
                eV.inc(v.bn_stats(stats6[:, lane + 6 * c: lane + 6 * c + 6],
                                  bass.AP(R, t * D + 512 * c, [[8 * D, 128], [1, 512]])))
            chunk_cnt[t] = eV.n
        blk.vector(f_c)

    def emit_stats(t, gate):
        emit_stats_chunk(t, 0, 8, gate)
        emit_stats_fin(t)

    def emit_stats_fin(t):
        blk = BLK()

        def f_v1(v, t=t):
            lane = (t % 4) * 48
            W(v, "v", eV, chunk_cnt[t])
            W(v, "v", eV, nmi_done[t])
            c = eV.inc(v.bn_aggr(mv[:, 2 * t:2 * t + 2], stats6[:, lane:lane + 48]))
            lane_free[t % 4] = c
            W(v, "v", eV, c)
            c = eV.inc(v.tensor_scalar_add(mv[:, 2 * t + 1:2 * t + 2],
                                           mv[:, 2 * t + 1:2 * t + 2], EPS))
            W(v, "v", eV, c)
            stats_done[t] = eV.inc(v.reciprocal(mv[:, 2 * t + 1:2 * t + 2],
                                                mv[:, 2 * t + 1:2 * t + 2]))
        blk.vector(f_v1)

        def f_s(s, t=t):
            W(s, "a", eV, stats_done[t])
            W(s, "a", eA, apply_done[t])
            istd_done[t] = eA.inc(s.activation(istd[:, t:t + 1],
                                               mv[:, 2 * t + 1:2 * t + 2], AF.Sqrt))
        blk.scalar(f_s)

        def f_v2(v, t=t):
            W(v, "v", eA, istd_done[t])
            nmi_done[t] = eV.inc(v.tensor_scalar(nmi[:, t:t + 1], mv[:, 2 * t:2 * t + 1],
                                                 -1.0, istd[:, t:t + 1], ALU.mult, ALU.mult))
        blk.vector(f_v2)

    def emit_apply_transpose(tiles, mm_guard):
        blk = BLK()
        for i, t in enumerate(tiles):
            xb = xnb[i % 2]
            for kh in range(2):
                def f_ap(s, t=t, xb=xb, kh=kh, i=i):
                    W(s, "a", eV, nmi_done[t])
                    W(s, "a", eA, istd_done[t])
                    W(s, "a", eT, xnb_guard[i % 2])
                    apply_done[t] = eA.inc(s.activation(
                        xb[:], bass.AP(R, t * D + 2048 * kh, [[8 * D, 128], [1, 2048]]),
                        AF.Identity, scale=istd[:, t:t + 1], bias=nmi[:, t:t + 1]))
                blk.scalar(f_ap)

                for kq in range(4):
                    b = bank_seq[0] % 3
                    bank_seq[0] += 1

                    def f_tr(t_, t=t, xb=xb, kh=kh, kq=kq, b=b):
                        W(t_, "t", sCst, idn_load[0])
                        W(t_, "t", eA, apply_done[t])
                        W(t_, "t", dq_free[b][0], dq_free[b][1])
                        ins = None
                        for j in range(4):
                            ins = t_.matmul(psDQ[:, 512 * b + 128 * j:512 * b + 128 * (j + 1)],
                                            xb[:, 128 * (4 * kq + j):128 * (4 * kq + j + 1)],
                                            idn_sb[:], start=True, stop=True)
                        tr_mm[(t, kh, kq)] = eT.inc(ins)
                    blk.tensor(f_tr)

                    def f_ev(v, t=t, kh=kh, kq=kq, b=b, i=i, mm_guard=mm_guard):
                        W(v, "v", eT, mm_guard)
                        W(v, "v", eT, tr_mm[(t, kh, kq)])
                        dst = bass.AP(xnT, 512 * (16 * kh + 4 * kq) + 128 * i,
                                      [[32 * 512, 128], [512, 4], [1, 128]])
                        src = psDQ[:, 512 * b:512 * (b + 1)].rearrange(
                            "p (a c) -> p a c", a=4)
                        dq_free[b] = (eV, eV.inc(v.tensor_copy(dst, src)))
                    blk.vector(f_ev)
                xnb_guard[i % 2] = eT.n
        return eV.n

    def emit_dq_quarter(qidx, scl_gate):
        if qidx >= 64:
            return
        og, ihq = qidx // 4, qidx % 4
        blk = BLK()
        qsl_by_h = [0, 0]
        if True:
            for oti in range(2):
                ot = 2 * og + oti
                def f_gpz(g, og=og, ihq=ihq, oti=oti, ot=ot):
                    for h in range(2):
                        zslot = z_rr[0] % 2
                        z_rr[0] += 1
                        W(g, "g", eT, zrow_guard[zslot])
                        sZr[zslot].dma(g.dma_start(
                            out=zrow[zslot][:],
                            in_=zr[ot:ot + 1, 1024 * ihq + 512 * h:
                                   1024 * ihq + 512 * (h + 1)]))
                        zrow_load[(og, ihq, oti, h)] = (zslot, sZr[zslot].n)
                blk.gpsimd(f_gpz)

                qslot = q_rr[0] % 3
                q_rr[0] += 1

                def f_gp(g, og=og, ihq=ihq, oti=oti, ot=ot, qslot=qslot):
                    W(g, "g", eT, qbf_guard[qslot])
                    sQbf[qslot].dma(g.dma_start(
                        out=qbf[qslot][:],
                        in_=qw[128 * ot:128 * (ot + 1),
                               1024 * ihq:1024 * (ihq + 1)]))
                    qbf_load[(og, ihq, oti)] = sQbf[qslot].n
                blk.gpsimd(f_gp)

                dslot = d_rr[0] % 2
                d_rr[0] += 1
                col0 = 32 * ot + 8 * ihq

                def f_dv(v, col0=col0, dslot=dslot, scl_gate=scl_gate):
                    W(v, "v", sScl, scl_gate)
                    W(v, "v", sCst, idn_load[0])
                    W(v, "v", eT, diag_guard[dslot])
                    dst = diagB[dslot].rearrange("p (a b) -> p a b", b=128)
                    in0 = bass.AP(idnh_sb, 0, [[128, 128], [0, 8], [1, 128]])
                    in1 = bass.AP(scl_sb, col0, [[1024, 128], [1, 8], [0, 128]])
                    diag_ready[dslot] = (eV, eV.inc(v.tensor_tensor(
                        dst, in0, in1, ALU.mult)))
                blk.vector(f_dv)
                dready = diag_ready[dslot]

                for h in range(2):
                    b = dqb[0] % 3
                    dqb[0] += 1

                    def f_t(t_, og=og, ihq=ihq, oti=oti, h=h, b=b, qslot=qslot,
                            dslot=dslot, dready=dready):
                        zslot, zval = zrow_load[(og, ihq, oti, h)]
                        W(t_, "t", sQbf[qslot], qbf_load[(og, ihq, oti)])
                        W(t_, "t", sZr[zslot], zval)
                        W(t_, "t", dq_free[b][0], dq_free[b][1])
                        sem, val = dready
                        W(t_, "t", sem, val)
                        t_.matmul(psDQ[:, 512 * b:512 * (b + 1)], halfc[:],
                                  zrow[zslot][0:1, 0:512],
                                  start=True, stop=False)
                        ins = None
                        for j in range(4):
                            ins = t_.matmul(
                                psDQ[:, 512 * b + 128 * j:512 * b + 128 * (j + 1)],
                                qbf[qslot][:, 512 * h + 128 * j:512 * h + 128 * (j + 1)],
                                diagB[dslot][:, 512 * h + 128 * j:512 * h + 128 * (j + 1)],
                                start=False, stop=(j == 3))
                        dq_mm[(og, ihq, oti, h)] = eT.inc(ins)
                        zrow_guard[zslot] = eT.n
                        if h == 1:
                            diag_guard[dslot] = eT.n
                            qbf_guard[qslot] = eT.n
                    blk.tensor(f_t)

                    slot = (4 * og + ihq) % 3

                    def f_ev(s, og=og, ihq=ihq, oti=oti, h=h, b=b, slot=slot):
                        W(s, "a", eT, dq_mm[(og, ihq, oti, h)])
                        W(s, "a", eT, slab_guard[slot])
                        W(s, "a", sWr3[slot], wr3_guard[slot])
                        dst = bass.AP(arena[slot], 1024 * h + 128 * oti,
                                      [[2048, 128], [256, 4], [1, 128]])
                        srcp = psDQ[:, 512 * b:512 * (b + 1)].rearrange(
                            "p (a c) -> p a c", a=4)
                        dq_free[b] = (eA, eA.inc(s.activation(dst, srcp, AF.Identity)))
                    blk.scalar(f_ev)
            arena_rdy[(og, ihq)] = eA.n
            slot = (4 * og + ihq) % 3

            def f_wr(s, og=og, ihq=ihq, slot=slot):
                W(s, "s", eA, arena_rdy[(og, ihq)])
                sWr3[slot].dma(s.dma_start(
                    out=bass.AP(scr_f, og * 128 * 8192 + 2048 * ihq,
                                [[8192, 128], [1, 2048]]),
                    in_=arena[slot][:]))
                wr3_guard[slot] = sWr3[slot].n
                wr_f[(og, ihq)] = (slot, sWr3[slot].n)
            blk.sync(f_wr)

    def emit_mask_load_one():
        blk = BLK()
        m = n_mask_emitted[0]
        nt, c = m // 4, m % 4
        slot = m % 2

        def f(eng, m=m, nt=nt, c=c, slot=slot):
            if m - 2 >= 0:
                W(eng, "s", eV, m_red[m - 2])
            q = sMb[slot]
            q.dma(eng.dma_start(out=mbuf[slot][:],
                                in_=mask[128 * nt:128 * (nt + 1),
                                         1024 * c:1024 * (c + 1)]))
            m_load[m] = (q, q.n)
        blk.sync(f)
        n_mask_emitted[0] += 1

    def emit_mask_work(nmax):
        blk = BLK()
        nmax = min(nmax, 128)
        while n_red_emitted[0] < nmax:
            while n_mask_emitted[0] < min(n_red_emitted[0] + 2, 128):
                emit_mask_load_one()
            m = n_red_emitted[0]
            nt, c = m // 4, m % 4
            slot = m % 2

            def f(v, m=m, nt=nt, c=c, slot=slot):
                q, val = m_load[m]
                W(v, "v", q, val)
                if nt >= 1 and c == 0:
                    W(v, "v", eT, bm_mm[nt - 1])
                with nc.allow_low_precision(
                        reason="mask 0/1 block sums <=32, exact in bf16"):
                    ins = v.tensor_reduce(
                        S1[0][:, 32 * c:32 * (c + 1)],
                        mbuf[slot].rearrange("p (a b) -> p a b", b=32),
                        axis=AX.X, op=ALU.add)
                m_red[m] = eV.inc(ins)
                if c == 3:
                    s1_done[nt] = eV.n
            blk.vector(f)

            if c == 3:
                def f_t(t_, nt=nt):
                    W(t_, "t", sCst, idn_load[0])
                    W(t_, "t", eV, s1_done[nt])
                    ins = t_.matmul(psBM[:, 0:128],
                                    nbsel_sb[:, 128 - 4 * nt:256 - 4 * nt],
                                    S1[0][:], start=(nt == 0), stop=(nt == 31))
                    bm_mm[nt] = eT.inc(ins)
                blk.tensor(f_t)
            n_red_emitted[0] += 1

    def emit_bm_finalize():
        blk = BLK()

        def f_v(v):
            W(v, "v", eT, bm_mm[31])
            eV.inc(v.tensor_single_scalar(bm_bf[:], psBM[:, 0:128], 0.0, ALU.is_gt))
        blk.vector(f_v)
        th = eV.n

        b0 = bank_seq[0] % 3
        bank_seq[0] += 1

        def f_t(t_, b0=b0, th=th):
            W(t_, "t", eV, th)
            W(t_, "t", dq_free[b0][0], dq_free[b0][1])
            bmtr[0] = eT.inc(t_.matmul(psDQ[:, 512 * b0:512 * b0 + 128],
                                       bm_bf[:], idn_sb[:], start=True, stop=True))
        blk.tensor(f_t)

        def f_ev(v, b0=b0):
            W(v, "v", eT, bmtr[0])
            dq_free[b0] = (eV, eV.inc(v.tensor_copy(bmT[:], psDQ[:, 512 * b0:512 * b0 + 128])))
        blk.vector(f_ev)
        bmT_done = eV.n

        for kt in range(32):
            pslot = kt % 2

            def f_ld(s, kt=kt, pslot=pslot):
                W(s, "a", eT, scope_guard.get("dq_all", 0))
                W(s, "a", eV, scope_guard.get("dq_allv", 0))
                W(s, "a", eV, scope_guard.get("mask_all", 0))
                if kt >= 2:
                    W(s, "a", eT, selp_mm[kt - 2])
                sSelp[pslot].dma(s.dma_start(out=selp2[pslot][:],
                                             in_=selpad_d[:, 128 * kt:128 * (kt + 1)]))
                selp_ld[kt] = sSelp[pslot].n
            blk.scalar(f_ld)

            b = bank_seq[0] % 3
            bank_seq[0] += 1

            def f_t(t_, kt=kt, pslot=pslot, b=b, bmT_done=bmT_done):
                W(t_, "t", sSelp[pslot], selp_ld[kt])
                W(t_, "t", eV, bmT_done)
                W(t_, "t", dq_free[b][0], dq_free[b][1])
                selp_mm[kt] = eT.inc(t_.matmul(psDQ[:, 512 * b:512 * b + 128],
                                               selp2[pslot][:], bmT[:],
                                               start=True, stop=True))
                selp_bank[kt] = b
            blk.tensor(f_t)

            def f_ev(v, kt=kt):
                b = selp_bank[kt]
                W(v, "v", eT, selp_mm[kt])
                dq_free[b] = (eV, eV.inc(v.tensor_copy(
                    bmsel_sb[:, 128 * kt:128 * (kt + 1)],
                    psDQ[:, 512 * b:512 * b + 128])))
            blk.vector(f_ev)
        bm_final[0] = eV.n

    def emit_attn_q(nmax):
        blk = BLK()
        while n_aq[0] < min(nmax, 128):
            ch = n_aq[0]
            og, hh = ch // 8, ch % 8
            wslot = ch % 3

            def f_gp(g, ch=ch, og=og, hh=hh, wslot=wslot):
                W(g, "g", eT, scope_guard.get("dq_all", 0))
                W(g, "g", eV, scope_guard.get("dq_allv", 0))
                W(g, "g", sWr_a[wslot], wr_a_guard[wslot])
                sWtb[wslot].dma(g.dma_start(
                    out=wtb[wslot][:],
                    in_=bass.AP(wt_t, og * 128 * 8192 + 1024 * hh,
                                [[8192, 128], [1, 1024]])))
                wtb_load[ch] = sWtb[wslot].n
            blk.gpsimd(f_gp)

            def f_v(v, ch=ch, og=og, hh=hh, wslot=wslot):
                W(v, "v", eV, bm_final[0])
                if ch >= 1:
                    W(v, "v", eV, mult_done[ch - 1])
                sel = bass.AP(bmsel_sb, 128 * 4 * hh + 8 * og,
                              [[4096, 128], [128, 4], [1, 8], [0, 32]])
                c = eV.inc(v.tensor_copy(
                    bexp[0].rearrange("p (a b c) -> p a b c", b=8, c=32), sel))
                W(v, "v", eV, c)
                W(v, "v", sWtb[wslot], wtb_load[ch])
                mult_done[ch] = eV.inc(v.tensor_tensor(
                    wtb[wslot][:], wtb[wslot][:], bexp[0][:], ALU.mult))
            blk.vector(f_v)

            def f_wr(s, ch=ch, og=og, hh=hh, wslot=wslot):
                W(s, "s", eV, mult_done[ch])
                sWr_a[wslot].dma(s.dma_start(
                    out=bass.AP(scr_a, og * 128 * 8192 + 1024 * hh,
                                [[8192, 128], [1, 1024]]),
                    in_=wtb[wslot][:]))
                wr_a[ch] = (wslot, sWr_a[wslot].n)
                wr_a_guard[wslot] = sWr_a[wslot].n
            blk.sync(f_wr)
            n_aq[0] += 1

    def emit_bias_reload(src, gate):
        blk = BLK()

        def f_gp(g, src=src, gate=gate):
            W(g, "g", eT, gate)
            sBias.dma(g.dma_start(out=bias_sb[:], in_=src))
            bias_load[0] = sBias.n
        blk.gpsimd(f_gp)

    def emit_mm_round(sub, half, og, xnT_gate, interleave=None):
        blk = BLK()
        bias_col = onec if sub == 1 else halfc
        pg = pg_seq[0] % 2
        pg_seq[0] += 1
        slots = [(slab_rr[0] + q) % 3 for q in range(4)]
        slab_rr[0] += 4
        qc = [0] * 4

        def load_q(q):
            slot = slots[q]
            if sub == 0 and half == 0:
                return
            scr = scr_f if sub != 1 else scr_a
            for par in range(2):
                qsem = sArena[slot] if par == 0 else sAr2[slot]
                key = "s" if par == 0 else "a"

                def f_ld(eng, sub=sub, half=half, og=og, q=q, slot=slot, scr=scr,
                         qsem=qsem, key=key, par=par):
                    W(eng, key, eT, slab_guard[slot])
                    W(eng, key, sWr3[slot], wr3_guard[slot])
                    if sub == 0:
                        ws, val = wr_f[(og, q)]
                        W(eng, key, sWr3[ws], val)
                    elif sub == 1:
                        hh = 2 * q + par
                        ws, val = wr_a[og * 8 + hh]
                        W(eng, key, sWr_a[ws], val)
                    qsem.dma(eng.dma_start(
                        out=arena[slot][:, 1024 * par:1024 * (par + 1)],
                        in_=bass.AP(scr, og * 128 * 8192 + 2048 * q + 1024 * par,
                                    [[8192, 128], [1, 1024]])))
                    slab_load[(sub, half, og, q, par)] = (qsem, qsem.n)
                if par == 0:
                    blk.sync(f_ld)
                else:
                    blk.scalar(f_ld)

        def mm_q(q, last):
            slot = slots[q]

            def f_t(t_, sub=sub, half=half, og=og, q=q, slot=slot, pg=pg,
                    xnT_gate=xnT_gate, last=last):
                if q == 0:
                    W(t_, "t", sBias, bias_load[0])
                    W(t_, "t", eV, ps_free[pg])
                    W(t_, "t", eV, xnT_gate)
                if sub == 0 and half == 0:
                    W(t_, "t", eA, arena_rdy[(og, q)])
                else:
                    for par in range(2):
                        ls, lv = slab_load[(sub, half, og, q, par)]
                        W(t_, "t", ls, lv)
                ins = None
                for kj in range(8):
                    kt = 8 * q + kj
                    for tb in range(4):
                        ins = t_.matmul(
                            psMM[:, 1024 * pg + 256 * tb:1024 * pg + 256 * (tb + 1)],
                            xnT[:, 512 * kt + 128 * tb:512 * kt + 128 * (tb + 1)],
                            arena[slot][:, 256 * kj:256 * (kj + 1)],
                            start=(kt == 0 and tb % 2 == 0), stop=False)
                if not last:
                    qc[q] = eT.inc(ins)
                else:
                    for tb in range(4):
                        ins = t_.matmul(
                            psMM[:, 1024 * pg + 256 * tb:1024 * pg + 256 * (tb + 1)],
                            bias_col[:], bias_sb[0:1, 256 * og:256 * (og + 1)],
                            start=False, stop=(tb % 2 == 1))
                    qc[q] = mm_done[(sub, half, og)] = eT.inc(ins)
                slab_guard[slot] = qc[q]
            blk.tensor(f_t)

        load_q(0)
        load_q(1)
        load_q(2)
        mm_q(0, False)
        if interleave:
            interleave(0)
        load_q(3)
        mm_q(1, False)
        if interleave:
            interleave(1)
        mm_q(2, False)
        if interleave:
            interleave(2)
        mm_q(3, True)
        if interleave:
            interleave(3)

        def f_v(v, sub=sub, half=half, og=og, pg=pg):
            W(v, "v", eT, mm_done[(sub, half, og)])
            for pr in range(2):
                t0 = 4 * half + 2 * pr
                dst = bass.AP(R, t0 * D + 256 * og, [[8 * D, 128], [D, 2], [1, 256]])
                src = psMM[:, 1024 * pg + 512 * pr:1024 * pg + 512 * (pr + 1)].rearrange(
                    "p (a c) -> p a c", a=2)
                eV.inc(v.tensor_tensor(dst, dst, src, ALU.add))
            ps_free[pg] = eV.n
            radd[(sub, half, og)] = eV.n
            if og == 15:
                for t in range(4 * half, 4 * half + 4):
                    row_done[t] = eV.n
        blk.vector(f_v)

    DQL = 2

    stM = ExitStack()
    mbuf = [stM.enter_context(nc.sbuf_tensor(f"mb{i}", [128, 1024], mybir.dt.int8)) for i in range(2)]

    with nc.Block() as blk:
        holder["blk"] = blk
        scl_ready = emit_init()
        for t in range(8):
            emit_stats(t, xload[t])

        stB = ExitStack()
        qbf = [stB.enter_context(nc.sbuf_tensor(f"qbf{i}", [128, 1024], BF16))
               for i in range(3)]
        zrow = [stB.enter_context(nc.sbuf_tensor(f"zrow{i}", [1, 512], BF16))
                for i in range(2)]
        diagB = [stB.enter_context(nc.sbuf_tensor(f"diag{i}", [128, 1024], BF16))
                 for i in range(2)]
        scl_sb = stB.enter_context(nc.sbuf_tensor("scl_sb", [128, 1024], BF16))

        def f_sclld(g):
            sScl.dma(g.dma_start(out=scl_sb[:], in_=scl))
        blk.gpsimd(f_sclld)
        scl_ready = sScl.n

        emit_mask_work(2)

        for j in range(3):
            emit_dq_quarter(j, scl_ready)
        emit_apply_transpose([0, 1, 2, 3], None)
        xnT_gate = eV.n
        for og in range(16):
            emit_mm_round(0, 0, og, xnT_gate,
                          interleave=lambda q, og=og: emit_dq_quarter(
                              4 * og + q + 3, scl_ready))
            emit_mask_work(8 * (og + 1))
        scope_guard["dq_all"] = dq_mm[(15, 3, 1, 1)]
        scope_guard["dq_allv"] = eV.n
        stB.close()
        stM.close()
        stS = ExitStack()
        bmsel_sb = stS.enter_context(nc.sbuf_tensor("bmsel", [128, 4096], BF16))
        selp2 = [stS.enter_context(nc.sbuf_tensor(f"sp{i}", [128, 128], BF16))
                 for i in range(2)]
        stA = ExitStack()
        wtb = [stA.enter_context(nc.sbuf_tensor(f"wtb{i}", [128, 1024], BF16))
               for i in range(3)]
        bexp = [stA.enter_context(nc.sbuf_tensor("bexp0", [128, 1024], BF16))]

        emit_apply_transpose([4, 5, 6, 7], mm_done[(0, 0, 15)])
        xnT_gate = eV.n
        scope_guard["mask_all"] = m_red[127]
        emit_bm_finalize()

        next_c = [0] * 8
        for og in range(16):
            emit_mm_round(0, 1, og, xnT_gate)
            emit_attn_q(9 * (og + 1))
            if 2 <= og < 6:
                emit_stats(og - 2, (eV, row_done[og - 2]))
            if og >= 7:
                c_hi = min((og + 1) // 2, 8)
                for t in range(4, 8):
                    while next_c[t] < c_hi:
                        c = next_c[t]
                        emit_stats_chunk(t, c, c + 1,
                                         (eV, radd[(0, 1, min(15, 2 * c + 1))]))
                        next_c[t] += 1
        for t in range(4, 8):
            emit_stats_fin(t)

        emit_bias_reload(ab, mm_done[(0, 1, 15)])
        emit_apply_transpose([0, 1, 2, 3], mm_done[(0, 1, 15)])
        xnT_gate = eV.n
        for og in range(16):
            emit_mm_round(1, 0, og, xnT_gate)
        emit_apply_transpose([4, 5, 6, 7], mm_done[(1, 0, 15)])
        xnT_gate = eV.n
        next_c = [0] * 8
        for og in range(16):
            emit_mm_round(1, 1, og, xnT_gate)
            if 2 <= og < 6:
                emit_stats(og - 2, (eV, row_done[og - 2]))
            if og >= 7:
                c_hi = min((og + 1) // 2, 8)
                for t in range(4, 8):
                    while next_c[t] < c_hi:
                        c = next_c[t]
                        emit_stats_chunk(t, c, c + 1,
                                         (eV, radd[(1, 1, min(15, 2 * c + 1))]))
                        next_c[t] += 1
        for t in range(4, 8):
            emit_stats_fin(t)

        emit_bias_reload(fb, mm_done[(1, 1, 15)])
        emit_apply_transpose([0, 1, 2, 3], mm_done[(1, 1, 15)])
        xnT_gate = eV.n
        for og in range(16):
            emit_mm_round(2, 0, og, xnT_gate)
        emit_apply_transpose([4, 5, 6, 7], mm_done[(2, 0, 15)])
        xnT_gate = eV.n
        for og in range(16):
            emit_mm_round(2, 1, og, xnT_gate)
            if 8 <= og < 12:
                t = og - 8

                def f_y(sync, t=t):
                    W(sync, "s", eV, row_done[t])
                    sXY.dma(sync.dma_start(
                        out=y[128 * t:128 * (t + 1), :],
                        in_=bass.AP(R, t * D, [[8 * D, 128], [1, D]])))
                blk.sync(f_y)

        def f_yb(sync):
            for t in range(4, 8):
                W(sync, "s", eV, row_done[t])
                sXY.dma(sync.dma_start(
                    out=y[128 * t:128 * (t + 1), :],
                    in_=bass.AP(R, t * D, [[8 * D, 128], [1, D]])))
            W(sync, "s", sXY, sXY.n)
        blk.sync(f_yb)

    if stA is not None:
        stA.close()
    if stS is not None:
        stS.close()
    st.close()
    return nc


_CACHE = {}


def _get_nc():
    if "nc" not in _CACHE:
        _CACHE["nc"] = build_nc()
    return _CACHE["nc"]


def kernel(L0_attn_data, L0_attn_mask, L0_attn_weight, L0_attn_bias,
           L0_ffn_qweight, L0_ffn_scales, L0_ffn_scaled_zeros, L0_ffn_bias):
    x = np.ascontiguousarray(np.asarray(L0_attn_data, dtype=np.float32).reshape(8192, D))
    mask_np = np.ascontiguousarray(np.asarray(L0_attn_mask).astype(np.int8))
    wT = np.asarray(L0_attn_weight, dtype=np.float32).T
    wt_np = np.ascontiguousarray(
        wT.reshape(32, 128, 16, 256).transpose(2, 1, 0, 3).reshape(16, 128, 8192))
    ab_np = np.ascontiguousarray(np.asarray(L0_attn_bias, dtype=np.float32).reshape(1, D))
    qw_np = np.ascontiguousarray(np.asarray(L0_ffn_qweight).reshape(D, D).astype(np.int16))
    scl_np = np.ascontiguousarray(np.asarray(L0_ffn_scales, dtype=np.float32)
                                  .reshape(32, 128, 32).transpose(1, 0, 2).reshape(128, 1024))
    zr_np = np.ascontiguousarray(np.asarray(L0_ffn_scaled_zeros, dtype=np.float32)
                                 .reshape(32, 128, 32).transpose(0, 2, 1).reshape(32, D))
    fb_np = np.ascontiguousarray(np.asarray(L0_ffn_bias, dtype=np.float32).reshape(1, D))

    idn = np.eye(128).astype(ml_dtypes.bfloat16)
    idnh = (np.eye(128) * 0.5).astype(ml_dtypes.bfloat16)
    nbsel = np.zeros((128, 256), dtype=ml_dtypes.bfloat16)
    for p in range(128):
        nbsel[p, 128 + p // 32] = 1
    selpad = np.zeros((128, D), dtype=ml_dtypes.bfloat16)
    for kb in range(128):
        selpad[kb, 32 * kb:32 * (kb + 1)] = 1

    common = {"mask": mask_np, "wt_t": wt_np, "qw": qw_np, "scl": scl_np, "zr": zr_np,
              "ab": ab_np, "fb": fb_np, "idn": idn, "idnh": idnh, "nbsel": nbsel,
              "selpad": selpad}
    in_maps = [dict(common, x_in=np.ascontiguousarray(x[TOK * c:TOK * (c + 1)]))
               for c in range(8)]
    nc = _get_nc()
    res = run_bass_kernel_spmd(nc, in_maps, core_ids=list(range(8)))
    out = np.concatenate([np.asarray(res.results[c]["y"]) for c in range(8)], axis=0)
    return out.reshape(4, 2048, D).astype(np.float32)


# version 24
# speedup vs baseline: 1.1427x; 1.1427x over previous
import numpy as np
import ml_dtypes
from contextlib import ExitStack

import concourse.bass as bass
import concourse.mybir as mybir
from concourse.bass_utils import run_bass_kernel_spmd

F32 = mybir.dt.float32
BF16 = mybir.dt.bfloat16
I32 = mybir.dt.int32
AF = mybir.ActivationFunctionType
ALU = mybir.AluOpType
AX = mybir.AxisListType

D = 4096
TOK = 1024
EPS = 1e-5


class Sem:
    __slots__ = ("h", "n")

    def __init__(self, h):
        self.h = h
        self.n = 0

    def dma(self, instr):
        instr.then_inc(self.h, 16)
        self.n += 16
        return self.n

    def inc(self, instr):
        instr.then_inc(self.h, 1)
        self.n += 1
        return self.n


def build_nc():
    nc = bass.Bass("TRN2", target_bir_lowering=False)

    x_in = nc.dram_tensor("x_in", [TOK, D], F32, kind="ExternalInput").ap()
    mask = nc.dram_tensor("mask", [D, D], mybir.dt.int8, kind="ExternalInput").ap()
    wt_t = nc.dram_tensor("wt_t", [16, 128, 8192], F32, kind="ExternalInput")
    qw = nc.dram_tensor("qw", [D, D], mybir.dt.int16, kind="ExternalInput").ap()
    scl = nc.dram_tensor("scl", [128, 1024], F32, kind="ExternalInput").ap()
    zr = nc.dram_tensor("zr", [32, D], F32, kind="ExternalInput").ap()
    ab = nc.dram_tensor("ab", [1, D], F32, kind="ExternalInput").ap()
    fb = nc.dram_tensor("fb", [1, D], F32, kind="ExternalInput").ap()
    idn = nc.dram_tensor("idn", [128, 128], BF16, kind="ExternalInput").ap()
    idnh = nc.dram_tensor("idnh", [128, 128], BF16, kind="ExternalInput").ap()
    nbsel_d = nc.dram_tensor("nbsel", [128, 256], BF16, kind="ExternalInput").ap()
    selpad_d = nc.dram_tensor("selpad", [128, D], BF16, kind="ExternalInput").ap()
    y = nc.dram_tensor("y", [TOK, D], F32, kind="ExternalOutput").ap()
    scr_f = nc.dram_tensor("scr_f", [16, 128, 8192], BF16, kind="Internal")
    scr_a = nc.dram_tensor("scr_a", [16, 128, 8192], BF16, kind="Internal")

    st = ExitStack()
    E = st.enter_context
    R = E(nc.sbuf_tensor("R", [128, 8 * D], F32))
    xnT = E(nc.sbuf_tensor("xnT", [128, 32 * 512], BF16))
    arena = [E(nc.sbuf_tensor(f"slab{i}", [128, 2048], BF16)) for i in range(3)]
    xnb = [E(nc.sbuf_tensor(f"xnb{i}", [128, 2048], BF16)) for i in range(2)]
    bias_sb = E(nc.sbuf_tensor("bias_sb", [1, D], BF16))
    idn_sb = E(nc.sbuf_tensor("idn_sb", [128, 128], BF16))
    idnh_sb = E(nc.sbuf_tensor("idnh_sb", [128, 128], BF16))
    nbsel_sb = E(nc.sbuf_tensor("nbsel_sb", [128, 256], BF16))
    halfc = E(nc.sbuf_tensor("halfc", [1, 128], BF16))
    onec = E(nc.sbuf_tensor("onec", [1, 128], BF16))
    S1 = [E(nc.sbuf_tensor("S1a", [128, 128], BF16))]
    bm_bf = E(nc.sbuf_tensor("bm_bf", [128, 128], BF16))
    bmT = E(nc.sbuf_tensor("bmT", [128, 128], BF16))
    mv = E(nc.sbuf_tensor("mv", [128, 16], F32))
    stats6 = E(nc.sbuf_tensor("stats6", [128, 192], F32))
    istd = E(nc.sbuf_tensor("istd", [128, 8], F32))
    nmi = E(nc.sbuf_tensor("nmi", [128, 8], F32))

    psMM = E(nc.psum_tensor("psMM", [128, 2048], F32))
    psDQ = E(nc.psum_tensor("psDQ", [128, 1536], F32))
    psBM = E(nc.psum_tensor("psBM", [128, 512], F32))

    sQbf = [Sem(E(nc.semaphore(f"sQbf{i}"))) for i in range(3)]
    sZr = [Sem(E(nc.semaphore(f"sZr{i}"))) for i in range(2)]
    sWtb = [Sem(E(nc.semaphore(f"sWtb{i}"))) for i in range(3)]
    sGm = Sem(E(nc.semaphore("sGm")))
    sArena = [Sem(E(nc.semaphore(f"sAr{i}"))) for i in range(3)]
    sAr2 = [Sem(E(nc.semaphore(f"sAq{i}"))) for i in range(3)]
    sMb = [Sem(E(nc.semaphore(f"sMb{i}"))) for i in range(2)]
    sXY = Sem(E(nc.semaphore("sXY")))
    sCst = Sem(E(nc.semaphore("sCst")))
    sWr3 = [Sem(E(nc.semaphore(f"sWr{i}"))) for i in range(3)]
    sWr_a = [Sem(E(nc.semaphore(f"sWa{i}"))) for i in range(3)]
    sSelp = [Sem(E(nc.semaphore(f"sSp{i}"))) for i in range(2)]
    sScl = Sem(E(nc.semaphore("sScl")))
    sBias = Sem(E(nc.semaphore("sBias")))
    sX2 = Sem(E(nc.semaphore("sX2")))
    eT = Sem(E(nc.semaphore("eT")))
    eV = Sem(E(nc.semaphore("eV")))
    eA = Sem(E(nc.semaphore("eA")))

    hwm = {}

    def W(eng, key, sem, val):
        if val is None or val <= 0:
            return
        k = (key, id(sem))
        if val > hwm.get(k, 0):
            eng.wait_ge(sem.h, val)
            hwm[k] = val

    xload = [0] * 8
    idn_load = [0]
    bias_load = [0]
    chunk_cnt = [0] * 8
    lane_free = [0] * 4
    stats_done = [0] * 8
    istd_done = [0] * 8
    nmi_done = [0] * 8
    apply_done = [0] * 8
    xnb_guard = [0, 0]
    ps_free = [0, 0]
    dq_free = [(eV, 0), (eV, 0), (eV, 0)]
    row_done = [0] * 8
    qbf_load = {}
    qbf_guard = [0] * 3
    zrow_load = {}
    zrow_guard = [0, 0]
    diag_ready = [None] * 2
    diag_guard = [0] * 2
    wr3_guard = [0, 0, 0]
    arena_rdy = {}
    radd = {}
    wr_f = {}
    wr_a = {}
    wr_a_guard = [0, 0, 0]
    mult_done = {}
    m_load = {}
    m_red = [0] * 128
    s1_done = [0] * 32
    bm_mm = [0] * 32
    n_mask_emitted = [0]
    n_red_emitted = [0]
    bm_final = [0]
    wtb_load = {}
    n_aq = [0]
    slab_load = {}
    slab_guard = [0, 0, 0]
    mm_done = {}
    tr_mm = {}
    dq_mm = {}
    dq_slots = []
    dqb = [0]
    bank_seq = [0]
    q_rr = [0]
    z_rr = [0]
    d_rr = [0]
    pg_seq = [0]
    slab_rr = [0]
    selp_ld = [0] * 32
    selp_mm = [0] * 32
    selp_bank = [0] * 32
    bmtr = [None]
    scope_guard = {}
    holder = {}

    def BLK():
        return holder["blk"]


    def emit_init():
        blk = BLK()

        def f_sync(sync):
            sCst.dma(sync.dma_start(out=idn_sb[:], in_=idn))
            sCst.dma(sync.dma_start(out=idnh_sb[:], in_=idnh))
            sCst.dma(sync.dma_start(out=nbsel_sb[:], in_=nbsel_d))
            idn_load[0] = sCst.n
            for t in range(4):
                sXY.dma(sync.dma_start(
                    out=bass.AP(R, t * D, [[8 * D, 128], [1, D]]),
                    in_=x_in[128 * t:128 * (t + 1), :]))
            for t in range(4):
                xload[t] = (sXY, sXY.n)
        blk.sync(f_sync)

        def f_sc(s):
            for t in range(4, 8):
                sX2.dma(s.dma_start(
                    out=bass.AP(R, t * D, [[8 * D, 128], [1, D]]),
                    in_=x_in[128 * t:128 * (t + 1), :]))
            for t in range(4, 8):
                xload[t] = (sX2, sX2.n)
        blk.scalar(f_sc)

        def f_gp(g):
            sBias.dma(g.dma_start(out=bias_sb[:], in_=fb))
            bias_load[0] = sBias.n
        blk.gpsimd(f_gp)

        def f_vec(v):
            eV.inc(v.memset(halfc[:], 0.5))
            eV.inc(v.memset(onec[:], 1.0))
        blk.vector(f_vec)
        return sScl.n

    def emit_stats_chunk(t, c0, c1, gate):
        blk = BLK()

        def f_c(v, t=t, c0=c0, c1=c1, gate=gate):
            if gate is not None:
                W(v, "v", gate[0], gate[1])
            if c0 == 0:
                W(v, "v", eV, lane_free[t % 4])
            lane = (t % 4) * 48
            for c in range(c0, c1):
                eV.inc(v.bn_stats(stats6[:, lane + 6 * c: lane + 6 * c + 6],
                                  bass.AP(R, t * D + 512 * c, [[8 * D, 128], [1, 512]])))
            chunk_cnt[t] = eV.n
        blk.vector(f_c)

    def emit_stats(t, gate):
        emit_stats_chunk(t, 0, 8, gate)
        emit_stats_fin(t)

    def emit_stats_fin(t):
        blk = BLK()

        def f_v1(v, t=t):
            lane = (t % 4) * 48
            W(v, "v", eV, chunk_cnt[t])
            W(v, "v", eV, nmi_done[t])
            c = eV.inc(v.bn_aggr(mv[:, 2 * t:2 * t + 2], stats6[:, lane:lane + 48]))
            lane_free[t % 4] = c
            W(v, "v", eV, c)
            c = eV.inc(v.tensor_scalar_add(mv[:, 2 * t + 1:2 * t + 2],
                                           mv[:, 2 * t + 1:2 * t + 2], EPS))
            W(v, "v", eV, c)
            stats_done[t] = eV.inc(v.reciprocal(mv[:, 2 * t + 1:2 * t + 2],
                                                mv[:, 2 * t + 1:2 * t + 2]))
        blk.vector(f_v1)

        def f_s(s, t=t):
            W(s, "a", eV, stats_done[t])
            W(s, "a", eA, apply_done[t])
            istd_done[t] = eA.inc(s.activation(istd[:, t:t + 1],
                                               mv[:, 2 * t + 1:2 * t + 2], AF.Sqrt))
        blk.scalar(f_s)

        def f_v2(v, t=t):
            W(v, "v", eA, istd_done[t])
            nmi_done[t] = eV.inc(v.tensor_scalar(nmi[:, t:t + 1], mv[:, 2 * t:2 * t + 1],
                                                 -1.0, istd[:, t:t + 1], ALU.mult, ALU.mult))
        blk.vector(f_v2)

    def emit_apply_transpose(tiles, mm_guard):
        blk = BLK()
        for i, t in enumerate(tiles):
            xb = xnb[i % 2]
            for kh in range(2):
                def f_ap(s, t=t, xb=xb, kh=kh, i=i):
                    W(s, "a", eV, nmi_done[t])
                    W(s, "a", eA, istd_done[t])
                    W(s, "a", eT, xnb_guard[i % 2])
                    apply_done[t] = eA.inc(s.activation(
                        xb[:], bass.AP(R, t * D + 2048 * kh, [[8 * D, 128], [1, 2048]]),
                        AF.Identity, scale=istd[:, t:t + 1], bias=nmi[:, t:t + 1]))
                blk.scalar(f_ap)

                for kq in range(4):
                    b = bank_seq[0] % 3
                    bank_seq[0] += 1

                    def f_tr(t_, t=t, xb=xb, kh=kh, kq=kq, b=b):
                        W(t_, "t", sCst, idn_load[0])
                        W(t_, "t", eA, apply_done[t])
                        W(t_, "t", dq_free[b][0], dq_free[b][1])
                        ins = None
                        for j in range(4):
                            ins = t_.matmul(psDQ[:, 512 * b + 128 * j:512 * b + 128 * (j + 1)],
                                            xb[:, 128 * (4 * kq + j):128 * (4 * kq + j + 1)],
                                            idn_sb[:], start=True, stop=True)
                        tr_mm[(t, kh, kq)] = eT.inc(ins)
                    blk.tensor(f_tr)

                    def f_ev(v, t=t, kh=kh, kq=kq, b=b, i=i, mm_guard=mm_guard):
                        W(v, "v", eT, mm_guard)
                        W(v, "v", eT, tr_mm[(t, kh, kq)])
                        dst = bass.AP(xnT, 512 * (16 * kh + 4 * kq) + 128 * i,
                                      [[32 * 512, 128], [512, 4], [1, 128]])
                        src = psDQ[:, 512 * b:512 * (b + 1)].rearrange(
                            "p (a c) -> p a c", a=4)
                        dq_free[b] = (eV, eV.inc(v.tensor_copy(dst, src)))
                    blk.vector(f_ev)
                xnb_guard[i % 2] = eT.n
        return eV.n

    def emit_dq_quarter(qidx, scl_gate):
        if qidx >= 64:
            return
        og, ihq = qidx // 4, qidx % 4
        blk = BLK()
        qsl_by_h = [0, 0]
        if True:
            for oti in range(2):
                ot = 2 * og + oti
                def f_gpz(g, og=og, ihq=ihq, oti=oti, ot=ot):
                    for h in range(2):
                        zslot = z_rr[0] % 2
                        z_rr[0] += 1
                        W(g, "g", eT, zrow_guard[zslot])
                        sZr[zslot].dma(g.dma_start(
                            out=zrow[zslot][:],
                            in_=zr[ot:ot + 1, 1024 * ihq + 512 * h:
                                   1024 * ihq + 512 * (h + 1)]))
                        zrow_load[(og, ihq, oti, h)] = (zslot, sZr[zslot].n)
                blk.gpsimd(f_gpz)

                qslot = q_rr[0] % 3
                q_rr[0] += 1

                def f_gp(g, og=og, ihq=ihq, oti=oti, ot=ot, qslot=qslot):
                    W(g, "g", eT, qbf_guard[qslot])
                    sQbf[qslot].dma(g.dma_start(
                        out=qbf[qslot][:],
                        in_=qw[128 * ot:128 * (ot + 1),
                               1024 * ihq:1024 * (ihq + 1)]))
                    qbf_load[(og, ihq, oti)] = sQbf[qslot].n
                blk.gpsimd(f_gp)

                dslot = d_rr[0] % 2
                d_rr[0] += 1
                col0 = 32 * ot + 8 * ihq

                def f_dv(v, col0=col0, dslot=dslot, scl_gate=scl_gate):
                    W(v, "v", sScl, scl_gate)
                    W(v, "v", sCst, idn_load[0])
                    W(v, "v", eT, diag_guard[dslot])
                    dst = diagB[dslot].rearrange("p (a b) -> p a b", b=128)
                    in0 = bass.AP(idnh_sb, 0, [[128, 128], [0, 8], [1, 128]])
                    in1 = bass.AP(scl_sb, col0, [[1024, 128], [1, 8], [0, 128]])
                    diag_ready[dslot] = (eV, eV.inc(v.tensor_tensor(
                        dst, in0, in1, ALU.mult)))
                blk.vector(f_dv)
                dready = diag_ready[dslot]

                for h in range(2):
                    b = dqb[0] % 3
                    dqb[0] += 1

                    def f_t(t_, og=og, ihq=ihq, oti=oti, h=h, b=b, qslot=qslot,
                            dslot=dslot, dready=dready):
                        zslot, zval = zrow_load[(og, ihq, oti, h)]
                        W(t_, "t", sQbf[qslot], qbf_load[(og, ihq, oti)])
                        W(t_, "t", sZr[zslot], zval)
                        W(t_, "t", dq_free[b][0], dq_free[b][1])
                        sem, val = dready
                        W(t_, "t", sem, val)
                        t_.matmul(psDQ[:, 512 * b:512 * (b + 1)], halfc[:],
                                  zrow[zslot][0:1, 0:512],
                                  start=True, stop=False)
                        ins = None
                        for j in range(4):
                            ins = t_.matmul(
                                psDQ[:, 512 * b + 128 * j:512 * b + 128 * (j + 1)],
                                qbf[qslot][:, 512 * h + 128 * j:512 * h + 128 * (j + 1)],
                                diagB[dslot][:, 512 * h + 128 * j:512 * h + 128 * (j + 1)],
                                start=False, stop=(j == 3))
                        dq_mm[(og, ihq, oti, h)] = eT.inc(ins)
                        zrow_guard[zslot] = eT.n
                        if h == 1:
                            diag_guard[dslot] = eT.n
                            qbf_guard[qslot] = eT.n
                    blk.tensor(f_t)

                    slot = (4 * og + ihq) % 3

                    def f_ev(s, og=og, ihq=ihq, oti=oti, h=h, b=b, slot=slot):
                        W(s, "a", eT, dq_mm[(og, ihq, oti, h)])
                        W(s, "a", eT, slab_guard[slot])
                        W(s, "a", sWr3[slot], wr3_guard[slot])
                        dst = bass.AP(arena[slot], 1024 * h + 128 * oti,
                                      [[2048, 128], [256, 4], [1, 128]])
                        srcp = psDQ[:, 512 * b:512 * (b + 1)].rearrange(
                            "p (a c) -> p a c", a=4)
                        dq_free[b] = (eA, eA.inc(s.activation(dst, srcp, AF.Identity)))
                    blk.scalar(f_ev)
            arena_rdy[(og, ihq)] = eA.n
            slot = (4 * og + ihq) % 3

            def f_wr(s, og=og, ihq=ihq, slot=slot):
                W(s, "a", eA, arena_rdy[(og, ihq)])
                sWr3[slot].dma(s.dma_start(
                    out=bass.AP(scr_f, og * 128 * 8192 + 2048 * ihq,
                                [[8192, 128], [1, 2048]]),
                    in_=arena[slot][:]))
                wr3_guard[slot] = sWr3[slot].n
                wr_f[(og, ihq)] = (slot, sWr3[slot].n)
            blk.scalar(f_wr)

    def emit_mask_load_one():
        blk = BLK()
        m = n_mask_emitted[0]
        nt, c = m // 4, m % 4
        slot = m % 2

        def f(eng, m=m, nt=nt, c=c, slot=slot):
            if m - 2 >= 0:
                W(eng, "s", eV, m_red[m - 2])
            q = sMb[slot]
            q.dma(eng.dma_start(out=mbuf[slot][:],
                                in_=mask[128 * nt:128 * (nt + 1),
                                         1024 * c:1024 * (c + 1)]))
            m_load[m] = (q, q.n)
        blk.sync(f)
        n_mask_emitted[0] += 1

    def emit_mask_work(nmax):
        blk = BLK()
        nmax = min(nmax, 128)
        while n_red_emitted[0] < nmax:
            while n_mask_emitted[0] < min(n_red_emitted[0] + 2, 128):
                emit_mask_load_one()
            m = n_red_emitted[0]
            nt, c = m // 4, m % 4
            slot = m % 2

            def f(v, m=m, nt=nt, c=c, slot=slot):
                q, val = m_load[m]
                W(v, "v", q, val)
                if nt >= 1 and c == 0:
                    W(v, "v", eT, bm_mm[nt - 1])
                with nc.allow_low_precision(
                        reason="mask 0/1 block sums <=32, exact in bf16"):
                    ins = v.tensor_reduce(
                        S1[0][:, 32 * c:32 * (c + 1)],
                        mbuf[slot].rearrange("p (a b) -> p a b", b=32),
                        axis=AX.X, op=ALU.add)
                m_red[m] = eV.inc(ins)
                if c == 3:
                    s1_done[nt] = eV.n
            blk.vector(f)

            if c == 3:
                def f_t(t_, nt=nt):
                    W(t_, "t", sCst, idn_load[0])
                    W(t_, "t", eV, s1_done[nt])
                    ins = t_.matmul(psBM[:, 0:128],
                                    nbsel_sb[:, 128 - 4 * nt:256 - 4 * nt],
                                    S1[0][:], start=(nt == 0), stop=(nt == 31))
                    bm_mm[nt] = eT.inc(ins)
                blk.tensor(f_t)
            n_red_emitted[0] += 1

    def emit_bm_finalize():
        blk = BLK()

        def f_v(v):
            W(v, "v", eT, bm_mm[31])
            eV.inc(v.tensor_single_scalar(bm_bf[:], psBM[:, 0:128], 0.0, ALU.is_gt))
        blk.vector(f_v)
        th = eV.n

        b0 = bank_seq[0] % 3
        bank_seq[0] += 1

        def f_t(t_, b0=b0, th=th):
            W(t_, "t", eV, th)
            W(t_, "t", dq_free[b0][0], dq_free[b0][1])
            bmtr[0] = eT.inc(t_.matmul(psDQ[:, 512 * b0:512 * b0 + 128],
                                       bm_bf[:], idn_sb[:], start=True, stop=True))
        blk.tensor(f_t)

        def f_ev(v, b0=b0):
            W(v, "v", eT, bmtr[0])
            dq_free[b0] = (eV, eV.inc(v.tensor_copy(bmT[:], psDQ[:, 512 * b0:512 * b0 + 128])))
        blk.vector(f_ev)
        bmT_done = eV.n

        for kt in range(32):
            pslot = kt % 2

            def f_ld(s, kt=kt, pslot=pslot):
                W(s, "a", eT, scope_guard.get("dq_all", 0))
                W(s, "a", eV, scope_guard.get("dq_allv", 0))
                W(s, "a", eV, scope_guard.get("mask_all", 0))
                if kt >= 2:
                    W(s, "a", eT, selp_mm[kt - 2])
                sSelp[pslot].dma(s.dma_start(out=selp2[pslot][:],
                                             in_=selpad_d[:, 128 * kt:128 * (kt + 1)]))
                selp_ld[kt] = sSelp[pslot].n
            blk.scalar(f_ld)

            b = bank_seq[0] % 3
            bank_seq[0] += 1

            def f_t(t_, kt=kt, pslot=pslot, b=b, bmT_done=bmT_done):
                W(t_, "t", sSelp[pslot], selp_ld[kt])
                W(t_, "t", eV, bmT_done)
                W(t_, "t", dq_free[b][0], dq_free[b][1])
                selp_mm[kt] = eT.inc(t_.matmul(psDQ[:, 512 * b:512 * b + 128],
                                               selp2[pslot][:], bmT[:],
                                               start=True, stop=True))
                selp_bank[kt] = b
            blk.tensor(f_t)

            def f_ev(v, kt=kt):
                b = selp_bank[kt]
                W(v, "v", eT, selp_mm[kt])
                dq_free[b] = (eV, eV.inc(v.tensor_copy(
                    bmsel_sb[:, 128 * kt:128 * (kt + 1)],
                    psDQ[:, 512 * b:512 * b + 128])))
            blk.vector(f_ev)
        bm_final[0] = eV.n

    def emit_attn_q(nmax):
        blk = BLK()
        while n_aq[0] < min(nmax, 128):
            ch = n_aq[0]
            og, hh = ch // 8, ch % 8
            wslot = ch % 3

            def f_gp(g, ch=ch, og=og, hh=hh, wslot=wslot):
                W(g, "g", eT, scope_guard.get("dq_all", 0))
                W(g, "g", eV, scope_guard.get("dq_allv", 0))
                W(g, "g", sWr_a[wslot], wr_a_guard[wslot])
                sWtb[wslot].dma(g.dma_start(
                    out=wtb[wslot][:],
                    in_=bass.AP(wt_t, og * 128 * 8192 + 1024 * hh,
                                [[8192, 128], [1, 1024]])))
                wtb_load[ch] = sWtb[wslot].n
            blk.gpsimd(f_gp)

            def f_v(v, ch=ch, og=og, hh=hh, wslot=wslot):
                W(v, "v", eV, bm_final[0])
                if ch >= 1:
                    W(v, "v", eV, mult_done[ch - 1])
                sel = bass.AP(bmsel_sb, 128 * 4 * hh + 8 * og,
                              [[4096, 128], [128, 4], [1, 8], [0, 32]])
                c = eV.inc(v.tensor_copy(
                    bexp[0].rearrange("p (a b c) -> p a b c", b=8, c=32), sel))
                W(v, "v", eV, c)
                W(v, "v", sWtb[wslot], wtb_load[ch])
                mult_done[ch] = eV.inc(v.tensor_tensor(
                    wtb[wslot][:], wtb[wslot][:], bexp[0][:], ALU.mult))
            blk.vector(f_v)

            def f_wr(s, ch=ch, og=og, hh=hh, wslot=wslot):
                W(s, "a", eV, mult_done[ch])
                sWr_a[wslot].dma(s.dma_start(
                    out=bass.AP(scr_a, og * 128 * 8192 + 1024 * hh,
                                [[8192, 128], [1, 1024]]),
                    in_=wtb[wslot][:]))
                wr_a[ch] = (wslot, sWr_a[wslot].n)
                wr_a_guard[wslot] = sWr_a[wslot].n
            blk.scalar(f_wr)
            n_aq[0] += 1

    def emit_bias_reload(src, gate):
        blk = BLK()

        def f_gp(g, src=src, gate=gate):
            W(g, "g", eT, gate)
            sBias.dma(g.dma_start(out=bias_sb[:], in_=src))
            bias_load[0] = sBias.n
        blk.gpsimd(f_gp)

    def emit_mm_round(sub, half, og, xnT_gate, interleave=None):
        blk = BLK()
        bias_col = onec if sub == 1 else halfc
        pg = pg_seq[0] % 2
        pg_seq[0] += 1
        slots = [(slab_rr[0] + q) % 3 for q in range(4)]
        slab_rr[0] += 4
        qc = [0] * 4

        def load_q(q):
            slot = slots[q]
            if sub == 0 and half == 0:
                return
            scr = scr_f if sub != 1 else scr_a

            def f_ld(eng, sub=sub, half=half, og=og, q=q, slot=slot, scr=scr):
                W(eng, "s", eT, slab_guard[slot])
                W(eng, "s", sWr3[slot], wr3_guard[slot])
                if sub == 0:
                    ws, val = wr_f[(og, q)]
                    W(eng, "s", sWr3[ws], val)
                elif sub == 1:
                    for hh in (2 * q, 2 * q + 1):
                        ws, val = wr_a[og * 8 + hh]
                        W(eng, "s", sWr_a[ws], val)
                sArena[slot].dma(eng.dma_start(
                    out=arena[slot][:],
                    in_=bass.AP(scr, og * 128 * 8192 + 2048 * q,
                                [[8192, 128], [1, 2048]])))
                slab_load[(sub, half, og, q)] = (sArena[slot], sArena[slot].n)
            blk.sync(f_ld)

        def mm_q(q, last):
            slot = slots[q]

            def f_t(t_, sub=sub, half=half, og=og, q=q, slot=slot, pg=pg,
                    xnT_gate=xnT_gate, last=last):
                if q == 0:
                    W(t_, "t", sBias, bias_load[0])
                    W(t_, "t", eV, ps_free[pg])
                    W(t_, "t", eV, xnT_gate)
                if sub == 0 and half == 0:
                    W(t_, "t", eA, arena_rdy[(og, q)])
                else:
                    ls, lv = slab_load[(sub, half, og, q)]
                    W(t_, "t", ls, lv)
                ins = None
                for kj in range(8):
                    kt = 8 * q + kj
                    for tb in range(4):
                        ins = t_.matmul(
                            psMM[:, 1024 * pg + 256 * tb:1024 * pg + 256 * (tb + 1)],
                            xnT[:, 512 * kt + 128 * tb:512 * kt + 128 * (tb + 1)],
                            arena[slot][:, 256 * kj:256 * (kj + 1)],
                            start=(kt == 0 and tb % 2 == 0), stop=False)
                if not last:
                    qc[q] = eT.inc(ins)
                else:
                    for tb in range(4):
                        ins = t_.matmul(
                            psMM[:, 1024 * pg + 256 * tb:1024 * pg + 256 * (tb + 1)],
                            bias_col[:], bias_sb[0:1, 256 * og:256 * (og + 1)],
                            start=False, stop=(tb % 2 == 1))
                    qc[q] = mm_done[(sub, half, og)] = eT.inc(ins)
                slab_guard[slot] = qc[q]
            blk.tensor(f_t)

        load_q(0)
        load_q(1)
        load_q(2)
        mm_q(0, False)
        if interleave:
            interleave(0)
        load_q(3)
        mm_q(1, False)
        if interleave:
            interleave(1)
        mm_q(2, False)
        if interleave:
            interleave(2)
        mm_q(3, True)
        if interleave:
            interleave(3)

        def f_v(v, sub=sub, half=half, og=og, pg=pg):
            W(v, "v", eT, mm_done[(sub, half, og)])
            for pr in range(2):
                t0 = 4 * half + 2 * pr
                dst = bass.AP(R, t0 * D + 256 * og, [[8 * D, 128], [D, 2], [1, 256]])
                src = psMM[:, 1024 * pg + 512 * pr:1024 * pg + 512 * (pr + 1)].rearrange(
                    "p (a c) -> p a c", a=2)
                eV.inc(v.tensor_tensor(dst, dst, src, ALU.add))
            ps_free[pg] = eV.n
            radd[(sub, half, og)] = eV.n
            if og == 15:
                for t in range(4 * half, 4 * half + 4):
                    row_done[t] = eV.n
        blk.vector(f_v)

    DQL = 2

    stM = ExitStack()
    mbuf = [stM.enter_context(nc.sbuf_tensor(f"mb{i}", [128, 1024], mybir.dt.int8)) for i in range(2)]

    with nc.Block() as blk:
        holder["blk"] = blk
        scl_ready = emit_init()
        for t in range(8):
            emit_stats(t, xload[t])

        stB = ExitStack()
        qbf = [stB.enter_context(nc.sbuf_tensor(f"qbf{i}", [128, 1024], BF16))
               for i in range(3)]
        zrow = [stB.enter_context(nc.sbuf_tensor(f"zrow{i}", [1, 512], BF16))
                for i in range(2)]
        diagB = [stB.enter_context(nc.sbuf_tensor(f"diag{i}", [128, 1024], BF16))
                 for i in range(2)]
        scl_sb = stB.enter_context(nc.sbuf_tensor("scl_sb", [128, 1024], BF16))

        def f_sclld(g):
            sScl.dma(g.dma_start(out=scl_sb[:], in_=scl))
        blk.gpsimd(f_sclld)
        scl_ready = sScl.n

        emit_mask_work(2)

        for j in range(3):
            emit_dq_quarter(j, scl_ready)
        emit_apply_transpose([0, 1, 2, 3], None)
        xnT_gate = eV.n
        for og in range(16):
            emit_mm_round(0, 0, og, xnT_gate,
                          interleave=lambda q, og=og: emit_dq_quarter(
                              4 * og + q + 3, scl_ready))
            emit_mask_work(8 * (og + 1))
        scope_guard["dq_all"] = dq_mm[(15, 3, 1, 1)]
        scope_guard["dq_allv"] = eV.n
        stB.close()
        stM.close()
        stS = ExitStack()
        bmsel_sb = stS.enter_context(nc.sbuf_tensor("bmsel", [128, 4096], BF16))
        selp2 = [stS.enter_context(nc.sbuf_tensor(f"sp{i}", [128, 128], BF16))
                 for i in range(2)]
        stA = ExitStack()
        wtb = [stA.enter_context(nc.sbuf_tensor(f"wtb{i}", [128, 1024], BF16))
               for i in range(3)]
        bexp = [stA.enter_context(nc.sbuf_tensor("bexp0", [128, 1024], BF16))]

        emit_apply_transpose([4, 5, 6, 7], mm_done[(0, 0, 15)])
        xnT_gate = eV.n
        scope_guard["mask_all"] = m_red[127]
        emit_bm_finalize()

        next_c = [0] * 8
        for og in range(16):
            emit_mm_round(0, 1, og, xnT_gate)
            emit_attn_q(9 * (og + 1))
            if 2 <= og < 6:
                emit_stats(og - 2, (eV, row_done[og - 2]))
            if og >= 7:
                c_hi = min((og + 1) // 2, 8)
                for t in range(4, 8):
                    while next_c[t] < c_hi:
                        c = next_c[t]
                        emit_stats_chunk(t, c, c + 1,
                                         (eV, radd[(0, 1, min(15, 2 * c + 1))]))
                        next_c[t] += 1
        for t in range(4, 8):
            emit_stats_fin(t)

        emit_bias_reload(ab, mm_done[(0, 1, 15)])
        emit_apply_transpose([0, 1, 2, 3], mm_done[(0, 1, 15)])
        xnT_gate = eV.n
        for og in range(16):
            emit_mm_round(1, 0, og, xnT_gate)
        emit_apply_transpose([4, 5, 6, 7], mm_done[(1, 0, 15)])
        xnT_gate = eV.n
        next_c = [0] * 8
        for og in range(16):
            emit_mm_round(1, 1, og, xnT_gate)
            if 2 <= og < 6:
                emit_stats(og - 2, (eV, row_done[og - 2]))
            if og >= 7:
                c_hi = min((og + 1) // 2, 8)
                for t in range(4, 8):
                    while next_c[t] < c_hi:
                        c = next_c[t]
                        emit_stats_chunk(t, c, c + 1,
                                         (eV, radd[(1, 1, min(15, 2 * c + 1))]))
                        next_c[t] += 1
        for t in range(4, 8):
            emit_stats_fin(t)

        emit_bias_reload(fb, mm_done[(1, 1, 15)])
        emit_apply_transpose([0, 1, 2, 3], mm_done[(1, 1, 15)])
        xnT_gate = eV.n
        for og in range(16):
            emit_mm_round(2, 0, og, xnT_gate)
        emit_apply_transpose([4, 5, 6, 7], mm_done[(2, 0, 15)])
        xnT_gate = eV.n
        for og in range(16):
            emit_mm_round(2, 1, og, xnT_gate)
            if 8 <= og < 12:
                t = og - 8

                def f_y(sync, t=t):
                    W(sync, "s", eV, row_done[t])
                    sXY.dma(sync.dma_start(
                        out=y[128 * t:128 * (t + 1), :],
                        in_=bass.AP(R, t * D, [[8 * D, 128], [1, D]])))
                blk.sync(f_y)

        def f_yb(sync):
            for t in range(4, 8):
                W(sync, "s", eV, row_done[t])
                sXY.dma(sync.dma_start(
                    out=y[128 * t:128 * (t + 1), :],
                    in_=bass.AP(R, t * D, [[8 * D, 128], [1, D]])))
            W(sync, "s", sXY, sXY.n)
        blk.sync(f_yb)

    if stA is not None:
        stA.close()
    if stS is not None:
        stS.close()
    st.close()
    return nc


_CACHE = {}


def _get_nc():
    if "nc" not in _CACHE:
        _CACHE["nc"] = build_nc()
    return _CACHE["nc"]


def kernel(L0_attn_data, L0_attn_mask, L0_attn_weight, L0_attn_bias,
           L0_ffn_qweight, L0_ffn_scales, L0_ffn_scaled_zeros, L0_ffn_bias):
    x = np.ascontiguousarray(np.asarray(L0_attn_data, dtype=np.float32).reshape(8192, D))
    mask_np = np.ascontiguousarray(np.asarray(L0_attn_mask).astype(np.int8))
    wT = np.asarray(L0_attn_weight, dtype=np.float32).T
    wt_np = np.ascontiguousarray(
        wT.reshape(32, 128, 16, 256).transpose(2, 1, 0, 3).reshape(16, 128, 8192))
    ab_np = np.ascontiguousarray(np.asarray(L0_attn_bias, dtype=np.float32).reshape(1, D))
    qw_np = np.ascontiguousarray(np.asarray(L0_ffn_qweight).reshape(D, D).astype(np.int16))
    scl_np = np.ascontiguousarray(np.asarray(L0_ffn_scales, dtype=np.float32)
                                  .reshape(32, 128, 32).transpose(1, 0, 2).reshape(128, 1024))
    zr_np = np.ascontiguousarray(np.asarray(L0_ffn_scaled_zeros, dtype=np.float32)
                                 .reshape(32, 128, 32).transpose(0, 2, 1).reshape(32, D))
    fb_np = np.ascontiguousarray(np.asarray(L0_ffn_bias, dtype=np.float32).reshape(1, D))

    idn = np.eye(128).astype(ml_dtypes.bfloat16)
    idnh = (np.eye(128) * 0.5).astype(ml_dtypes.bfloat16)
    nbsel = np.zeros((128, 256), dtype=ml_dtypes.bfloat16)
    for p in range(128):
        nbsel[p, 128 + p // 32] = 1
    selpad = np.zeros((128, D), dtype=ml_dtypes.bfloat16)
    for kb in range(128):
        selpad[kb, 32 * kb:32 * (kb + 1)] = 1

    common = {"mask": mask_np, "wt_t": wt_np, "qw": qw_np, "scl": scl_np, "zr": zr_np,
              "ab": ab_np, "fb": fb_np, "idn": idn, "idnh": idnh, "nbsel": nbsel,
              "selpad": selpad}
    in_maps = [dict(common, x_in=np.ascontiguousarray(x[TOK * c:TOK * (c + 1)]))
               for c in range(8)]
    nc = _get_nc()
    res = run_bass_kernel_spmd(nc, in_maps, core_ids=list(range(8)))
    out = np.concatenate([np.asarray(res.results[c]["y"]) for c in range(8)], axis=0)
    return out.reshape(4, 2048, D).astype(np.float32)


# version 25 (final)
# speedup vs baseline: 1.2452x; 1.0896x over previous
import numpy as np
import ml_dtypes
from contextlib import ExitStack

import concourse.bass as bass
import concourse.mybir as mybir
from concourse.bass_utils import run_bass_kernel_spmd

F32 = mybir.dt.float32
BF16 = mybir.dt.bfloat16
I32 = mybir.dt.int32
AF = mybir.ActivationFunctionType
ALU = mybir.AluOpType
AX = mybir.AxisListType

D = 4096
TOK = 1024
EPS = 1e-5


class Sem:
    __slots__ = ("h", "n")

    def __init__(self, h):
        self.h = h
        self.n = 0

    def dma(self, instr):
        instr.then_inc(self.h, 16)
        self.n += 16
        return self.n

    def inc(self, instr):
        instr.then_inc(self.h, 1)
        self.n += 1
        return self.n


def build_nc():
    nc = bass.Bass("TRN2", target_bir_lowering=False)

    x_in = nc.dram_tensor("x_in", [TOK, D], F32, kind="ExternalInput").ap()
    mask = nc.dram_tensor("mask", [D, D], mybir.dt.int8, kind="ExternalInput").ap()
    wt_t = nc.dram_tensor("wt_t", [16, 128, 8192], F32, kind="ExternalInput")
    qw = nc.dram_tensor("qw", [D, D], mybir.dt.int16, kind="ExternalInput").ap()
    scl = nc.dram_tensor("scl", [128, 1024], F32, kind="ExternalInput").ap()
    zr = nc.dram_tensor("zr", [32, D], F32, kind="ExternalInput").ap()
    ab = nc.dram_tensor("ab", [1, D], F32, kind="ExternalInput").ap()
    fb = nc.dram_tensor("fb", [1, D], F32, kind="ExternalInput").ap()
    idn = nc.dram_tensor("idn", [128, 128], BF16, kind="ExternalInput").ap()
    idnh = nc.dram_tensor("idnh", [128, 128], BF16, kind="ExternalInput").ap()
    nbsel_d = nc.dram_tensor("nbsel", [128, 256], BF16, kind="ExternalInput").ap()
    selpad_d = nc.dram_tensor("selpad", [128, D], BF16, kind="ExternalInput").ap()
    y = nc.dram_tensor("y", [TOK, D], F32, kind="ExternalOutput").ap()
    scr_f = nc.dram_tensor("scr_f", [16, 128, 8192], BF16, kind="Internal")
    scr_a = nc.dram_tensor("scr_a", [16, 128, 8192], BF16, kind="Internal")

    st = ExitStack()
    E = st.enter_context
    R = E(nc.sbuf_tensor("R", [128, 8 * D], F32))
    xnT = E(nc.sbuf_tensor("xnT", [128, 32 * 512], BF16))
    arena = [E(nc.sbuf_tensor(f"slab{i}", [128, 2048], BF16)) for i in range(3)]
    xnb = [E(nc.sbuf_tensor(f"xnb{i}", [128, 2048], BF16)) for i in range(2)]
    bias_sb = E(nc.sbuf_tensor("bias_sb", [1, D], BF16))
    idn_sb = E(nc.sbuf_tensor("idn_sb", [128, 128], BF16))
    idnh_sb = E(nc.sbuf_tensor("idnh_sb", [128, 128], BF16))
    nbsel_sb = E(nc.sbuf_tensor("nbsel_sb", [128, 256], BF16))
    halfc = E(nc.sbuf_tensor("halfc", [1, 128], BF16))
    onec = E(nc.sbuf_tensor("onec", [1, 128], BF16))
    S1 = [E(nc.sbuf_tensor("S1a", [128, 128], BF16))]
    bm_bf = E(nc.sbuf_tensor("bm_bf", [128, 128], BF16))
    bmT = E(nc.sbuf_tensor("bmT", [128, 128], BF16))
    mv = E(nc.sbuf_tensor("mv", [128, 16], F32))
    stats6 = E(nc.sbuf_tensor("stats6", [128, 192], F32))
    istd = E(nc.sbuf_tensor("istd", [128, 8], F32))
    nmi = E(nc.sbuf_tensor("nmi", [128, 8], F32))

    psMM = E(nc.psum_tensor("psMM", [128, 2048], F32))
    psDQ = E(nc.psum_tensor("psDQ", [128, 1536], F32))
    psBM = E(nc.psum_tensor("psBM", [128, 512], F32))

    sQbf = [Sem(E(nc.semaphore(f"sQbf{i}"))) for i in range(3)]
    sZr = [Sem(E(nc.semaphore(f"sZr{i}"))) for i in range(2)]
    sWtb = [Sem(E(nc.semaphore(f"sWtb{i}"))) for i in range(3)]
    sGm = Sem(E(nc.semaphore("sGm")))
    sArena = [Sem(E(nc.semaphore(f"sAr{i}"))) for i in range(3)]
    sAr2 = [Sem(E(nc.semaphore(f"sAq{i}"))) for i in range(3)]
    sMb = [Sem(E(nc.semaphore(f"sMb{i}"))) for i in range(2)]
    sXY = Sem(E(nc.semaphore("sXY")))
    sCst = Sem(E(nc.semaphore("sCst")))
    sWr3 = [Sem(E(nc.semaphore(f"sWr{i}"))) for i in range(3)]
    sWr_a = [Sem(E(nc.semaphore(f"sWa{i}"))) for i in range(3)]
    sSelp = [Sem(E(nc.semaphore(f"sSp{i}"))) for i in range(2)]
    sScl = Sem(E(nc.semaphore("sScl")))
    sBias = Sem(E(nc.semaphore("sBias")))
    sX2 = Sem(E(nc.semaphore("sX2")))
    eT = Sem(E(nc.semaphore("eT")))
    eV = Sem(E(nc.semaphore("eV")))
    eA = Sem(E(nc.semaphore("eA")))

    hwm = {}

    def W(eng, key, sem, val):
        if val is None or val <= 0:
            return
        k = (key, id(sem))
        if val > hwm.get(k, 0):
            eng.wait_ge(sem.h, val)
            hwm[k] = val

    xload = [0] * 8
    idn_load = [0]
    bias_load = [0]
    chunk_cnt = [0] * 8
    lane_free = [0] * 4
    stats_done = [0] * 8
    istd_done = [0] * 8
    nmi_done = [0] * 8
    apply_done = [0] * 8
    xnb_guard = [0, 0]
    ps_free = [0, 0]
    dq_free = [(eV, 0), (eV, 0), (eV, 0)]
    row_done = [0] * 8
    qbf_load = {}
    qbf_guard = [0] * 3
    zrow_load = {}
    zrow_guard = [0, 0]
    diag_ready = [None] * 2
    diag_guard = [0] * 2
    wr3_guard = [0, 0, 0]
    arena_rdy = {}
    radd = {}
    wr_f = {}
    wr_a = {}
    wr_a_guard = [0, 0, 0]
    mult_done = {}
    m_load = {}
    m_red = [0] * 128
    s1_done = [0] * 32
    bm_mm = [0] * 32
    n_mask_emitted = [0]
    n_red_emitted = [0]
    bm_final = [0]
    wtb_load = {}
    n_aq = [0]
    slab_load = {}
    slab_guard = [0, 0, 0]
    mm_done = {}
    tr_mm = {}
    dq_mm = {}
    dq_slots = []
    dqb = [0]
    bank_seq = [0]
    q_rr = [0]
    z_rr = [0]
    d_rr = [0]
    pg_seq = [0]
    slab_rr = [0]
    selp_ld = [0] * 32
    selp_mm = [0] * 32
    selp_bank = [0] * 32
    bmtr = [None]
    scope_guard = {}
    holder = {}

    def BLK():
        return holder["blk"]


    def emit_init():
        blk = BLK()

        def f_sync(sync):
            sCst.dma(sync.dma_start(out=idn_sb[:], in_=idn))
            sCst.dma(sync.dma_start(out=idnh_sb[:], in_=idnh))
            sCst.dma(sync.dma_start(out=nbsel_sb[:], in_=nbsel_d))
            idn_load[0] = sCst.n
            for t in range(4):
                sXY.dma(sync.dma_start(
                    out=bass.AP(R, t * D, [[8 * D, 128], [1, D]]),
                    in_=x_in[128 * t:128 * (t + 1), :]))
            for t in range(4):
                xload[t] = (sXY, sXY.n)
        blk.sync(f_sync)

        def f_sc(s):
            for t in range(4, 8):
                sX2.dma(s.dma_start(
                    out=bass.AP(R, t * D, [[8 * D, 128], [1, D]]),
                    in_=x_in[128 * t:128 * (t + 1), :]))
            for t in range(4, 8):
                xload[t] = (sX2, sX2.n)
        blk.scalar(f_sc)

        def f_gp(g):
            sBias.dma(g.dma_start(out=bias_sb[:], in_=fb))
            bias_load[0] = sBias.n
        blk.gpsimd(f_gp)

        def f_vec(v):
            eV.inc(v.memset(halfc[:], 0.5))
            eV.inc(v.memset(onec[:], 1.0))
        blk.vector(f_vec)
        return sScl.n

    def emit_stats_chunk(t, c0, c1, gate):
        blk = BLK()

        def f_c(v, t=t, c0=c0, c1=c1, gate=gate):
            if gate is not None:
                W(v, "v", gate[0], gate[1])
            if c0 == 0:
                W(v, "v", eV, lane_free[t % 4])
            lane = (t % 4) * 48
            for c in range(c0, c1):
                eV.inc(v.bn_stats(stats6[:, lane + 6 * c: lane + 6 * c + 6],
                                  bass.AP(R, t * D + 512 * c, [[8 * D, 128], [1, 512]])))
            chunk_cnt[t] = eV.n
        blk.vector(f_c)

    def emit_stats(t, gate):
        emit_stats_chunk(t, 0, 8, gate)
        emit_stats_fin(t)

    def emit_stats_fin(t):
        blk = BLK()

        def f_v1(v, t=t):
            lane = (t % 4) * 48
            W(v, "v", eV, chunk_cnt[t])
            W(v, "v", eV, nmi_done[t])
            c = eV.inc(v.bn_aggr(mv[:, 2 * t:2 * t + 2], stats6[:, lane:lane + 48]))
            lane_free[t % 4] = c
            W(v, "v", eV, c)
            c = eV.inc(v.tensor_scalar_add(mv[:, 2 * t + 1:2 * t + 2],
                                           mv[:, 2 * t + 1:2 * t + 2], EPS))
            W(v, "v", eV, c)
            stats_done[t] = eV.inc(v.reciprocal(mv[:, 2 * t + 1:2 * t + 2],
                                                mv[:, 2 * t + 1:2 * t + 2]))
        blk.vector(f_v1)

        def f_s(s, t=t):
            W(s, "a", eV, stats_done[t])
            W(s, "a", eA, apply_done[t])
            istd_done[t] = eA.inc(s.activation(istd[:, t:t + 1],
                                               mv[:, 2 * t + 1:2 * t + 2], AF.Sqrt))
        blk.scalar(f_s)

        def f_v2(v, t=t):
            W(v, "v", eA, istd_done[t])
            nmi_done[t] = eV.inc(v.tensor_scalar(nmi[:, t:t + 1], mv[:, 2 * t:2 * t + 1],
                                                 -1.0, istd[:, t:t + 1], ALU.mult, ALU.mult))
        blk.vector(f_v2)

    def emit_apply_transpose(tiles, mm_guard):
        blk = BLK()
        for i, t in enumerate(tiles):
            xb = xnb[i % 2]
            for kh in range(2):
                def f_ap(s, t=t, xb=xb, kh=kh, i=i):
                    W(s, "a", eV, nmi_done[t])
                    W(s, "a", eA, istd_done[t])
                    W(s, "a", eT, xnb_guard[i % 2])
                    apply_done[t] = eA.inc(s.activation(
                        xb[:], bass.AP(R, t * D + 2048 * kh, [[8 * D, 128], [1, 2048]]),
                        AF.Identity, scale=istd[:, t:t + 1], bias=nmi[:, t:t + 1]))
                blk.scalar(f_ap)

                for kq in range(4):
                    b = bank_seq[0] % 3
                    bank_seq[0] += 1

                    def f_tr(t_, t=t, xb=xb, kh=kh, kq=kq, b=b):
                        W(t_, "t", sCst, idn_load[0])
                        W(t_, "t", eA, apply_done[t])
                        W(t_, "t", dq_free[b][0], dq_free[b][1])
                        ins = None
                        for j in range(4):
                            ins = t_.matmul(psDQ[:, 512 * b + 128 * j:512 * b + 128 * (j + 1)],
                                            xb[:, 128 * (4 * kq + j):128 * (4 * kq + j + 1)],
                                            idn_sb[:], start=True, stop=True)
                        tr_mm[(t, kh, kq)] = eT.inc(ins)
                    blk.tensor(f_tr)

                    def f_ev(v, t=t, kh=kh, kq=kq, b=b, i=i, mm_guard=mm_guard):
                        W(v, "v", eT, mm_guard)
                        W(v, "v", eT, tr_mm[(t, kh, kq)])
                        dst = bass.AP(xnT, 512 * (16 * kh + 4 * kq) + 128 * i,
                                      [[32 * 512, 128], [512, 4], [1, 128]])
                        src = psDQ[:, 512 * b:512 * (b + 1)].rearrange(
                            "p (a c) -> p a c", a=4)
                        dq_free[b] = (eV, eV.inc(v.tensor_copy(dst, src)))
                    blk.vector(f_ev)
                xnb_guard[i % 2] = eT.n
        return eV.n

    def emit_dq_quarter(qidx, scl_gate):
        if qidx >= 64:
            return
        og, ihq = qidx // 4, qidx % 4
        blk = BLK()
        qsl_by_h = [0, 0]
        if True:
            for oti in range(2):
                ot = 2 * og + oti
                qslot = q_rr[0] % 3
                q_rr[0] += 1

                def f_gp(g, og=og, ihq=ihq, oti=oti, ot=ot, qslot=qslot):
                    W(g, "g", eT, qbf_guard[qslot])
                    sQbf[qslot].dma(g.dma_start(
                        out=qbf[qslot][:],
                        in_=qw[128 * ot:128 * (ot + 1),
                               1024 * ihq:1024 * (ihq + 1)]))
                    qbf_load[(og, ihq, oti)] = sQbf[qslot].n
                blk.gpsimd(f_gp)

                def f_gpz(g, og=og, ihq=ihq, oti=oti, ot=ot):
                    for h in range(2):
                        zslot = z_rr[0] % 2
                        z_rr[0] += 1
                        W(g, "g", eT, zrow_guard[zslot])
                        sZr[zslot].dma(g.dma_start(
                            out=zrow[zslot][:],
                            in_=zr[ot:ot + 1, 1024 * ihq + 512 * h:
                                   1024 * ihq + 512 * (h + 1)]))
                        zrow_load[(og, ihq, oti, h)] = (zslot, sZr[zslot].n)
                blk.gpsimd(f_gpz)

                dslot = d_rr[0] % 2
                d_rr[0] += 1
                col0 = 32 * ot + 8 * ihq

                def f_dv(v, col0=col0, dslot=dslot, scl_gate=scl_gate):
                    W(v, "v", sScl, scl_gate)
                    W(v, "v", sCst, idn_load[0])
                    W(v, "v", eT, diag_guard[dslot])
                    dst = diagB[dslot].rearrange("p (a b) -> p a b", b=128)
                    in0 = bass.AP(idnh_sb, 0, [[128, 128], [0, 8], [1, 128]])
                    in1 = bass.AP(scl_sb, col0, [[1024, 128], [1, 8], [0, 128]])
                    diag_ready[dslot] = (eV, eV.inc(v.tensor_tensor(
                        dst, in0, in1, ALU.mult)))
                blk.vector(f_dv)
                dready = diag_ready[dslot]

                for h in range(2):
                    b = dqb[0] % 3
                    dqb[0] += 1

                    def f_t(t_, og=og, ihq=ihq, oti=oti, h=h, b=b, qslot=qslot,
                            dslot=dslot, dready=dready):
                        zslot, zval = zrow_load[(og, ihq, oti, h)]
                        W(t_, "t", sQbf[qslot], qbf_load[(og, ihq, oti)])
                        W(t_, "t", sZr[zslot], zval)
                        W(t_, "t", dq_free[b][0], dq_free[b][1])
                        sem, val = dready
                        W(t_, "t", sem, val)
                        t_.matmul(psDQ[:, 512 * b:512 * (b + 1)], halfc[:],
                                  zrow[zslot][0:1, 0:512],
                                  start=True, stop=False)
                        ins = None
                        for j in range(4):
                            ins = t_.matmul(
                                psDQ[:, 512 * b + 128 * j:512 * b + 128 * (j + 1)],
                                qbf[qslot][:, 512 * h + 128 * j:512 * h + 128 * (j + 1)],
                                diagB[dslot][:, 512 * h + 128 * j:512 * h + 128 * (j + 1)],
                                start=False, stop=(j == 3))
                        dq_mm[(og, ihq, oti, h)] = eT.inc(ins)
                        zrow_guard[zslot] = eT.n
                        if h == 1:
                            diag_guard[dslot] = eT.n
                            qbf_guard[qslot] = eT.n
                    blk.tensor(f_t)

                    slot = (4 * og + ihq) % 3

                    def f_ev(s, og=og, ihq=ihq, oti=oti, h=h, b=b, slot=slot):
                        W(s, "a", eT, dq_mm[(og, ihq, oti, h)])
                        W(s, "a", eT, slab_guard[slot])
                        W(s, "a", sWr3[slot], wr3_guard[slot])
                        dst = bass.AP(arena[slot], 1024 * h + 128 * oti,
                                      [[2048, 128], [256, 4], [1, 128]])
                        srcp = psDQ[:, 512 * b:512 * (b + 1)].rearrange(
                            "p (a c) -> p a c", a=4)
                        dq_free[b] = (eA, eA.inc(s.activation(dst, srcp, AF.Identity)))
                    blk.scalar(f_ev)
            arena_rdy[(og, ihq)] = eA.n
            slot = (4 * og + ihq) % 3

            def f_wr(s, og=og, ihq=ihq, slot=slot):
                W(s, "a", eA, arena_rdy[(og, ihq)])
                sWr3[slot].dma(s.dma_start(
                    out=bass.AP(scr_f, og * 128 * 8192 + 2048 * ihq,
                                [[8192, 128], [1, 2048]]),
                    in_=arena[slot][:]))
                wr3_guard[slot] = sWr3[slot].n
                wr_f[(og, ihq)] = (slot, sWr3[slot].n)
            blk.scalar(f_wr)

    def emit_mask_load_one():
        blk = BLK()
        m = n_mask_emitted[0]
        nt, c = m // 4, m % 4
        slot = m % 2

        def f(eng, m=m, nt=nt, c=c, slot=slot):
            if m - 2 >= 0:
                W(eng, "s", eV, m_red[m - 2])
            q = sMb[slot]
            q.dma(eng.dma_start(out=mbuf[slot][:],
                                in_=mask[128 * nt:128 * (nt + 1),
                                         1024 * c:1024 * (c + 1)]))
            m_load[m] = (q, q.n)
        blk.sync(f)
        n_mask_emitted[0] += 1

    def emit_mask_work(nmax):
        blk = BLK()
        nmax = min(nmax, 128)
        while n_red_emitted[0] < nmax:
            while n_mask_emitted[0] < min(n_red_emitted[0] + 2, 128):
                emit_mask_load_one()
            m = n_red_emitted[0]
            nt, c = m // 4, m % 4
            slot = m % 2

            def f(v, m=m, nt=nt, c=c, slot=slot):
                q, val = m_load[m]
                W(v, "v", q, val)
                if nt >= 1 and c == 0:
                    W(v, "v", eT, bm_mm[nt - 1])
                with nc.allow_low_precision(
                        reason="mask 0/1 block sums <=32, exact in bf16"):
                    ins = v.tensor_reduce(
                        S1[0][:, 32 * c:32 * (c + 1)],
                        mbuf[slot].rearrange("p (a b) -> p a b", b=32),
                        axis=AX.X, op=ALU.add)
                m_red[m] = eV.inc(ins)
                if c == 3:
                    s1_done[nt] = eV.n
            blk.vector(f)

            if c == 3:
                def f_t(t_, nt=nt):
                    W(t_, "t", sCst, idn_load[0])
                    W(t_, "t", eV, s1_done[nt])
                    ins = t_.matmul(psBM[:, 0:128],
                                    nbsel_sb[:, 128 - 4 * nt:256 - 4 * nt],
                                    S1[0][:], start=(nt == 0), stop=(nt == 31))
                    bm_mm[nt] = eT.inc(ins)
                blk.tensor(f_t)
            n_red_emitted[0] += 1

    def emit_bm_finalize():
        blk = BLK()

        def f_v(v):
            W(v, "v", eT, bm_mm[31])
            eV.inc(v.tensor_single_scalar(bm_bf[:], psBM[:, 0:128], 0.0, ALU.is_gt))
        blk.vector(f_v)
        th = eV.n

        b0 = bank_seq[0] % 3
        bank_seq[0] += 1

        def f_t(t_, b0=b0, th=th):
            W(t_, "t", eV, th)
            W(t_, "t", dq_free[b0][0], dq_free[b0][1])
            bmtr[0] = eT.inc(t_.matmul(psDQ[:, 512 * b0:512 * b0 + 128],
                                       bm_bf[:], idn_sb[:], start=True, stop=True))
        blk.tensor(f_t)

        def f_ev(v, b0=b0):
            W(v, "v", eT, bmtr[0])
            dq_free[b0] = (eV, eV.inc(v.tensor_copy(bmT[:], psDQ[:, 512 * b0:512 * b0 + 128])))
        blk.vector(f_ev)
        bmT_done = eV.n

        for kt in range(32):
            pslot = kt % 2

            def f_ld(s, kt=kt, pslot=pslot):
                W(s, "a", eT, scope_guard.get("dq_all", 0))
                W(s, "a", eV, scope_guard.get("dq_allv", 0))
                W(s, "a", eV, scope_guard.get("mask_all", 0))
                if kt >= 2:
                    W(s, "a", eT, selp_mm[kt - 2])
                sSelp[pslot].dma(s.dma_start(out=selp2[pslot][:],
                                             in_=selpad_d[:, 128 * kt:128 * (kt + 1)]))
                selp_ld[kt] = sSelp[pslot].n
            blk.scalar(f_ld)

            b = bank_seq[0] % 3
            bank_seq[0] += 1

            def f_t(t_, kt=kt, pslot=pslot, b=b, bmT_done=bmT_done):
                W(t_, "t", sSelp[pslot], selp_ld[kt])
                W(t_, "t", eV, bmT_done)
                W(t_, "t", dq_free[b][0], dq_free[b][1])
                selp_mm[kt] = eT.inc(t_.matmul(psDQ[:, 512 * b:512 * b + 128],
                                               selp2[pslot][:], bmT[:],
                                               start=True, stop=True))
                selp_bank[kt] = b
            blk.tensor(f_t)

            def f_ev(v, kt=kt):
                b = selp_bank[kt]
                W(v, "v", eT, selp_mm[kt])
                dq_free[b] = (eV, eV.inc(v.tensor_copy(
                    bmsel_sb[:, 128 * kt:128 * (kt + 1)],
                    psDQ[:, 512 * b:512 * b + 128])))
            blk.vector(f_ev)
        bm_final[0] = eV.n

    def emit_attn_q(nmax):
        blk = BLK()
        while n_aq[0] < min(nmax, 128):
            ch = n_aq[0]
            og, hh = ch // 8, ch % 8
            wslot = ch % 3

            def f_gp(g, ch=ch, og=og, hh=hh, wslot=wslot):
                W(g, "g", eT, scope_guard.get("dq_all", 0))
                W(g, "g", eV, scope_guard.get("dq_allv", 0))
                W(g, "g", sWr_a[wslot], wr_a_guard[wslot])
                sWtb[wslot].dma(g.dma_start(
                    out=wtb[wslot][:],
                    in_=bass.AP(wt_t, og * 128 * 8192 + 1024 * hh,
                                [[8192, 128], [1, 1024]])))
                wtb_load[ch] = sWtb[wslot].n
            blk.gpsimd(f_gp)

            def f_v(v, ch=ch, og=og, hh=hh, wslot=wslot):
                W(v, "v", eV, bm_final[0])
                if ch >= 1:
                    W(v, "v", eV, mult_done[ch - 1])
                sel = bass.AP(bmsel_sb, 128 * 4 * hh + 8 * og,
                              [[4096, 128], [128, 4], [1, 8], [0, 32]])
                c = eV.inc(v.tensor_copy(
                    bexp[0].rearrange("p (a b c) -> p a b c", b=8, c=32), sel))
                W(v, "v", eV, c)
                W(v, "v", sWtb[wslot], wtb_load[ch])
                mult_done[ch] = eV.inc(v.tensor_tensor(
                    wtb[wslot][:], wtb[wslot][:], bexp[0][:], ALU.mult))
            blk.vector(f_v)

            def f_wr(s, ch=ch, og=og, hh=hh, wslot=wslot):
                W(s, "a", eV, mult_done[ch])
                sWr_a[wslot].dma(s.dma_start(
                    out=bass.AP(scr_a, og * 128 * 8192 + 1024 * hh,
                                [[8192, 128], [1, 1024]]),
                    in_=wtb[wslot][:]))
                wr_a[ch] = (wslot, sWr_a[wslot].n)
                wr_a_guard[wslot] = sWr_a[wslot].n
            blk.scalar(f_wr)
            n_aq[0] += 1

    def emit_bias_reload(src, gate):
        blk = BLK()

        def f_gp(g, src=src, gate=gate):
            W(g, "g", eT, gate)
            sBias.dma(g.dma_start(out=bias_sb[:], in_=src))
            bias_load[0] = sBias.n
        blk.gpsimd(f_gp)

    def emit_mm_round(sub, half, og, xnT_gate, interleave=None):
        blk = BLK()
        bias_col = onec if sub == 1 else halfc
        pg = pg_seq[0] % 2
        pg_seq[0] += 1
        slots = [(slab_rr[0] + q) % 3 for q in range(4)]
        slab_rr[0] += 4
        qc = [0] * 4

        def load_q(q):
            slot = slots[q]
            if sub == 0 and half == 0:
                return
            scr = scr_f if sub != 1 else scr_a

            def f_ld(eng, sub=sub, half=half, og=og, q=q, slot=slot, scr=scr):
                W(eng, "s", eT, slab_guard[slot])
                W(eng, "s", sWr3[slot], wr3_guard[slot])
                if sub == 0:
                    ws, val = wr_f[(og, q)]
                    W(eng, "s", sWr3[ws], val)
                elif sub == 1:
                    for hh in (2 * q, 2 * q + 1):
                        ws, val = wr_a[og * 8 + hh]
                        W(eng, "s", sWr_a[ws], val)
                sArena[slot].dma(eng.dma_start(
                    out=arena[slot][:],
                    in_=bass.AP(scr, og * 128 * 8192 + 2048 * q,
                                [[8192, 128], [1, 2048]])))
                slab_load[(sub, half, og, q)] = (sArena[slot], sArena[slot].n)
            blk.sync(f_ld)

        def mm_q(q, last):
            slot = slots[q]

            def f_t(t_, sub=sub, half=half, og=og, q=q, slot=slot, pg=pg,
                    xnT_gate=xnT_gate, last=last):
                if q == 0:
                    W(t_, "t", sBias, bias_load[0])
                    W(t_, "t", eV, ps_free[pg])
                    W(t_, "t", eV, xnT_gate)
                if sub == 0 and half == 0:
                    W(t_, "t", eA, arena_rdy[(og, q)])
                else:
                    ls, lv = slab_load[(sub, half, og, q)]
                    W(t_, "t", ls, lv)
                ins = None
                for kj in range(8):
                    kt = 8 * q + kj
                    for tb in range(4):
                        ins = t_.matmul(
                            psMM[:, 1024 * pg + 256 * tb:1024 * pg + 256 * (tb + 1)],
                            xnT[:, 512 * kt + 128 * tb:512 * kt + 128 * (tb + 1)],
                            arena[slot][:, 256 * kj:256 * (kj + 1)],
                            start=(kt == 0 and tb % 2 == 0), stop=False)
                if not last:
                    qc[q] = eT.inc(ins)
                else:
                    for tb in range(4):
                        ins = t_.matmul(
                            psMM[:, 1024 * pg + 256 * tb:1024 * pg + 256 * (tb + 1)],
                            bias_col[:], bias_sb[0:1, 256 * og:256 * (og + 1)],
                            start=False, stop=(tb % 2 == 1))
                    qc[q] = mm_done[(sub, half, og)] = eT.inc(ins)
                slab_guard[slot] = qc[q]
            blk.tensor(f_t)

        load_q(0)
        load_q(1)
        load_q(2)
        mm_q(0, False)
        if interleave:
            interleave(0)
        load_q(3)
        mm_q(1, False)
        if interleave:
            interleave(1)
        mm_q(2, False)
        if interleave:
            interleave(2)
        mm_q(3, True)
        if interleave:
            interleave(3)

        def f_v(v, sub=sub, half=half, og=og, pg=pg):
            W(v, "v", eT, mm_done[(sub, half, og)])
            for pr in range(2):
                t0 = 4 * half + 2 * pr
                dst = bass.AP(R, t0 * D + 256 * og, [[8 * D, 128], [D, 2], [1, 256]])
                src = psMM[:, 1024 * pg + 512 * pr:1024 * pg + 512 * (pr + 1)].rearrange(
                    "p (a c) -> p a c", a=2)
                eV.inc(v.tensor_tensor(dst, dst, src, ALU.add))
            ps_free[pg] = eV.n
            radd[(sub, half, og)] = eV.n
            if og == 15:
                for t in range(4 * half, 4 * half + 4):
                    row_done[t] = eV.n
        blk.vector(f_v)

    DQL = 2

    stM = ExitStack()
    mbuf = [stM.enter_context(nc.sbuf_tensor(f"mb{i}", [128, 1024], mybir.dt.int8)) for i in range(2)]

    with nc.Block() as blk:
        holder["blk"] = blk
        scl_ready = emit_init()
        for t in range(8):
            emit_stats(t, xload[t])

        stB = ExitStack()
        qbf = [stB.enter_context(nc.sbuf_tensor(f"qbf{i}", [128, 1024], BF16))
               for i in range(3)]
        zrow = [stB.enter_context(nc.sbuf_tensor(f"zrow{i}", [1, 512], BF16))
                for i in range(2)]
        diagB = [stB.enter_context(nc.sbuf_tensor(f"diag{i}", [128, 1024], BF16))
                 for i in range(2)]
        scl_sb = stB.enter_context(nc.sbuf_tensor("scl_sb", [128, 1024], BF16))

        def f_sclld(g):
            sScl.dma(g.dma_start(out=scl_sb[:], in_=scl))
        blk.gpsimd(f_sclld)
        scl_ready = sScl.n

        emit_mask_work(2)

        for j in range(3):
            emit_dq_quarter(j, scl_ready)
        emit_apply_transpose([0, 1, 2, 3], None)
        xnT_gate = eV.n
        for og in range(16):
            emit_mm_round(0, 0, og, xnT_gate,
                          interleave=lambda q, og=og: emit_dq_quarter(
                              4 * og + q + 3, scl_ready))
            emit_mask_work(8 * (og + 1))
        scope_guard["dq_all"] = dq_mm[(15, 3, 1, 1)]
        scope_guard["dq_allv"] = eV.n
        stB.close()
        stM.close()
        stS = ExitStack()
        bmsel_sb = stS.enter_context(nc.sbuf_tensor("bmsel", [128, 4096], BF16))
        selp2 = [stS.enter_context(nc.sbuf_tensor(f"sp{i}", [128, 128], BF16))
                 for i in range(2)]
        stA = ExitStack()
        wtb = [stA.enter_context(nc.sbuf_tensor(f"wtb{i}", [128, 1024], BF16))
               for i in range(3)]
        bexp = [stA.enter_context(nc.sbuf_tensor("bexp0", [128, 1024], BF16))]

        emit_apply_transpose([4, 5, 6, 7], mm_done[(0, 0, 15)])
        xnT_gate = eV.n
        scope_guard["mask_all"] = m_red[127]
        emit_bm_finalize()

        next_c = [0] * 8
        for og in range(16):
            emit_mm_round(0, 1, og, xnT_gate)
            emit_attn_q(9 * (og + 1))
            if 2 <= og < 6:
                emit_stats(og - 2, (eV, row_done[og - 2]))
            if og >= 7:
                c_hi = min((og + 1) // 2, 8)
                for t in range(4, 8):
                    while next_c[t] < c_hi:
                        c = next_c[t]
                        emit_stats_chunk(t, c, c + 1,
                                         (eV, radd[(0, 1, min(15, 2 * c + 1))]))
                        next_c[t] += 1
        for t in range(4, 8):
            emit_stats_fin(t)

        emit_bias_reload(ab, mm_done[(0, 1, 15)])
        emit_apply_transpose([0, 1, 2, 3], mm_done[(0, 1, 15)])
        xnT_gate = eV.n
        for og in range(16):
            emit_mm_round(1, 0, og, xnT_gate)
        emit_apply_transpose([4, 5, 6, 7], mm_done[(1, 0, 15)])
        xnT_gate = eV.n
        next_c = [0] * 8
        for og in range(16):
            emit_mm_round(1, 1, og, xnT_gate)
            if 2 <= og < 6:
                emit_stats(og - 2, (eV, row_done[og - 2]))
            if og >= 7:
                c_hi = min((og + 1) // 2, 8)
                for t in range(4, 8):
                    while next_c[t] < c_hi:
                        c = next_c[t]
                        emit_stats_chunk(t, c, c + 1,
                                         (eV, radd[(1, 1, min(15, 2 * c + 1))]))
                        next_c[t] += 1
        for t in range(4, 8):
            emit_stats_fin(t)

        emit_bias_reload(fb, mm_done[(1, 1, 15)])
        emit_apply_transpose([0, 1, 2, 3], mm_done[(1, 1, 15)])
        xnT_gate = eV.n
        for og in range(16):
            emit_mm_round(2, 0, og, xnT_gate)
        emit_apply_transpose([4, 5, 6, 7], mm_done[(2, 0, 15)])
        xnT_gate = eV.n
        for og in range(16):
            emit_mm_round(2, 1, og, xnT_gate)
            if 8 <= og < 12:
                t = og - 8

                def f_y(sync, t=t):
                    W(sync, "s", eV, row_done[t])
                    sXY.dma(sync.dma_start(
                        out=y[128 * t:128 * (t + 1), :],
                        in_=bass.AP(R, t * D, [[8 * D, 128], [1, D]])))
                blk.sync(f_y)

        def f_yb(sync):
            for t in range(4, 8):
                W(sync, "s", eV, row_done[t])
                sXY.dma(sync.dma_start(
                    out=y[128 * t:128 * (t + 1), :],
                    in_=bass.AP(R, t * D, [[8 * D, 128], [1, D]])))
            W(sync, "s", sXY, sXY.n)
        blk.sync(f_yb)

    if stA is not None:
        stA.close()
    if stS is not None:
        stS.close()
    st.close()
    return nc


_CACHE = {}


def _get_nc():
    if "nc" not in _CACHE:
        _CACHE["nc"] = build_nc()
    return _CACHE["nc"]


def kernel(L0_attn_data, L0_attn_mask, L0_attn_weight, L0_attn_bias,
           L0_ffn_qweight, L0_ffn_scales, L0_ffn_scaled_zeros, L0_ffn_bias):
    x = np.ascontiguousarray(np.asarray(L0_attn_data, dtype=np.float32).reshape(8192, D))
    mask_np = np.ascontiguousarray(np.asarray(L0_attn_mask).astype(np.int8))
    wT = np.asarray(L0_attn_weight, dtype=np.float32).T
    wt_np = np.ascontiguousarray(
        wT.reshape(32, 128, 16, 256).transpose(2, 1, 0, 3).reshape(16, 128, 8192))
    ab_np = np.ascontiguousarray(np.asarray(L0_attn_bias, dtype=np.float32).reshape(1, D))
    qw_np = np.ascontiguousarray(np.asarray(L0_ffn_qweight).reshape(D, D).astype(np.int16))
    scl_np = np.ascontiguousarray(np.asarray(L0_ffn_scales, dtype=np.float32)
                                  .reshape(32, 128, 32).transpose(1, 0, 2).reshape(128, 1024))
    zr_np = np.ascontiguousarray(np.asarray(L0_ffn_scaled_zeros, dtype=np.float32)
                                 .reshape(32, 128, 32).transpose(0, 2, 1).reshape(32, D))
    fb_np = np.ascontiguousarray(np.asarray(L0_ffn_bias, dtype=np.float32).reshape(1, D))

    idn = np.eye(128).astype(ml_dtypes.bfloat16)
    idnh = (np.eye(128) * 0.5).astype(ml_dtypes.bfloat16)
    nbsel = np.zeros((128, 256), dtype=ml_dtypes.bfloat16)
    for p in range(128):
        nbsel[p, 128 + p // 32] = 1
    selpad = np.zeros((128, D), dtype=ml_dtypes.bfloat16)
    for kb in range(128):
        selpad[kb, 32 * kb:32 * (kb + 1)] = 1

    common = {"mask": mask_np, "wt_t": wt_np, "qw": qw_np, "scl": scl_np, "zr": zr_np,
              "ab": ab_np, "fb": fb_np, "idn": idn, "idnh": idnh, "nbsel": nbsel,
              "selpad": selpad}
    in_maps = [dict(common, x_in=np.ascontiguousarray(x[TOK * c:TOK * (c + 1)]))
               for c in range(8)]
    nc = _get_nc()
    res = run_bass_kernel_spmd(nc, in_maps, core_ids=list(range(8)))
    out = np.concatenate([np.asarray(res.results[c]["y"]) for c in range(8)], axis=0)
    return out.reshape(4, 2048, D).astype(np.float32)
